# version 11
# baseline (speedup 1.0000x reference)
import numpy as np
from contextlib import ExitStack

import concourse.bass as bass
import concourse.mybir as mybir
from concourse.bass_utils import run_bass_kernel_spmd

F32 = mybir.dt.float32
BF16 = mybir.dt.bfloat16
I32 = mybir.dt.int32
AX = mybir.AxisListType.X
OP = mybir.AluOpType
AF = mybir.ActivationFunctionType

L = 4096
D = 1024
DI = 2048
FI = 4096
T = 512
NCH = L // T
TPC = T // 128


class Sem:
    def __init__(self, es, nc, name):
        self.h = es.enter_context(nc.semaphore(name))
        self.n = 0

    def inc(self, instr, k=1):
        instr.then_inc(self.h, k)
        self.n += k
        return instr


class WStream:

    def __init__(self, es, nc, name, buf, slots, cols):
        self.buf, self.slots, self.cols = buf, slots, cols
        self.cons = []
        self.sems = [Sem(es, nc, f"{name}_s{i}") for i in range(slots)]
        self.first_waits = None

    def slot(self, i):
        c = (i % self.slots) * self.cols
        return self.buf[:, c:c + self.cols]


def build_program(stage="full", dbg=None):
    nc = bass.Bass("TRN2", target_bir_lowering=False,
                   detect_race_conditions=False)

    def din(name, shape, dtype=F32):
        return nc.dram_tensor(name, list(shape), dtype, kind="ExternalInput").ap()

    d_x = din("x_in", (L, D))
    d_win = din("w_in", (D, FI))
    d_wx = din("w_x", (DI, 32))
    d_wdtb = din("w_dtb", (48, DI))
    d_bdt = din("b_dt_col", (128, 16))
    d_dpar = din("d_col", (128, 16))
    d_conv = din("conv_col", (128, 64))
    d_wout = din("w_out", (DI, D))
    d_wg = din("w_gate", (FI, D))
    d_wu = din("w_up", (FI, D))
    d_wd = din("w_down", (D, FI))
    d_idf = din("idf", (128, 128))
    d_idb = din("idb", (128, 128), BF16)
    d_onerow = din("onerow", (1, 128))
    d_onecol = din("onecol", (128, 1))
    d_consts = din("consts", (128, 4))

    d_wgT = nc.dram_tensor("wgT", [D, FI], BF16).ap()
    d_wuT = nc.dram_tensor("wuT", [D, FI], BF16).ap()
    d_wdT = nc.dram_tensor("wdT", [FI, D], BF16).ap()
    d_winH = nc.dram_tensor("winH", [D, FI], BF16).ap()
    d_winL = nc.dram_tensor("winL", [D, FI], BF16).ap()
    d_y = nc.dram_tensor("y_out", [L, D], F32, kind="ExternalOutput").ap()
    d_dbg = nc.dram_tensor("dbg_out", [128, 16384], F32, kind="ExternalOutput").ap()

    es = ExitStack()
    s_gq = Sem(es, nc, "s_gq")
    s_sq = Sem(es, nc, "s_sq")
    s_pec = Sem(es, nc, "s_pec")
    s_mm = Sem(es, nc, "s_mm")
    s_ev = Sem(es, nc, "s_ev")
    s_v = Sem(es, nc, "s_v")
    s_sc = Sem(es, nc, "s_sc")
    s_st = Sem(es, nc, "s_st")

    _uid = [0]
    _dbg_ts = []

    def uname(n):
        _uid[0] += 1
        return f"{n}_{_uid[0]}"

    def sb(name, shape, dtype=F32):
        return es.enter_context(nc.sbuf_tensor(name, list(shape), dtype))

    X = sb("X", (128, TPC * D))
    STAT = sb("STAT", (128, 96))
    SCW = sb("SCW", (128, 8))
    H = sb("H", (128, 16))
    HIST = sb("HIST", (128, 48))
    BDT = sb("BDT", (128, 16))
    DPAR = sb("DPAR", (128, 16))
    CONVW = sb("CONVW", (128, 64))
    IDF = sb("IDF", (128, 128))
    IDB = sb("IDB", (128, 128), BF16)
    ONEROW = sb("ONEROW", (1, 128))
    ONECOL = sb("ONECOL", (128, 1))
    WXS = sb("WXS", (128, 512))
    WDTB = sb("WDTB", (48, DI))
    QADUM = sb("QADUM", (128, 1024))
    CONSTS = sb("CONSTS", (128, 4))

    block = es.enter_context(nc.Block())
    EN = dict(sync=block.sync, gpsimd=block.gpsimd, tensor=block.tensor,
              vector=block.vector, scalar=block.scalar)

    def emit(eng, fn):
        EN[eng](fn)

    C_MEAN, C_RSTD, C_AM, C_Q1, C_GSC, C_C2, C_S2, C_D1 = 0, 4, 8, 12, 16, 20, 24, 28
    C_C, C_VS, C_T1, C_T2, C_BND, C_AM2, C_S1 = 32, 36, 40, 44, 48, 52, 56

    def scol(base, tt):
        return STAT[:, base + tt:base + tt + 1]

    nqg = {"mm": 0}
    xlng = {"mm": 0}

    def vv(v, instr):
        s_v.inc(instr)
        v.wait_ge(s_v.h, s_v.n)

    def ss(s, instr):
        s_sc.inc(instr)
        s.wait_ge(s_sc.h, s_sc.n)

    def pewarm(t, PW, waits):
        for h, nv in waits:
            t.wait_ge(h, nv)
            t.matmul(PW[0:1, 0:1], ONECOL[:], ONECOL[:],
                     start=True, stop=True)

    def stream_loads(stream, tiles, n_groups, queue="sync", mm_final=None):
        pec_start = s_pec.n
        if mm_final is None:
            mm_final = s_mm.n + n_groups
        n = len(tiles)
        base_idx = len(stream.cons)
        for j in range(n - 1):
            stream.cons.append((s_pec.h, pec_start + j + 1))
        stream.cons.append((s_mm.h, mm_final))

        tile_waits = []
        cnt = {s: stream.sems[s].n for s in range(stream.slots)}
        for j in range(n):
            s = (base_idx + j) % stream.slots
            cnt[s] += 16
            tile_waits.append((stream.sems[s].h, cnt[s]))

        def f(q):
            if stream.first_waits is not None:
                for h, nv in stream.first_waits:
                    q.wait_ge(h, nv)
                stream.first_waits = None
            for j, t_ap in enumerate(tiles):
                gi = base_idx + j
                if gi >= stream.slots:
                    sem, val = stream.cons[gi - stream.slots]
                    q.wait_ge(sem, val)
                stream.sems[gi % stream.slots].inc(
                    q.dma_start(out=stream.slot(gi), in_=t_ap), 16)
        emit(queue, f)
        return tile_waits, base_idx

    def prologue_smalls():
        def f(g):
            for sbuf, dram in [(IDF, d_idf), (IDB, d_idb), (ONEROW, d_onerow),
                               (ONECOL, d_onecol), (BDT, d_bdt), (DPAR, d_dpar),
                               (CONVW, d_conv), (WDTB, d_wdtb),
                               (CONSTS, d_consts)]:
                s_gq.inc(g.dma_start(out=sbuf[:], in_=dram[:]), 16)
            for ci in range(16):
                s_gq.inc(g.dma_start(out=WXS[:, ci * 32:(ci + 1) * 32],
                                     in_=d_wx[ci * 128:(ci + 1) * 128, :]), 16)
        emit("gpsimd", f)
        gq_smalls = s_gq.n

        def w(e):
            e.wait_ge(s_gq.h, gq_smalls)
        for eng in ("tensor", "vector", "scalar"):
            emit(eng, w)

        def fv(v):
            v.memset(H[:], 0.0)
            v.memset(HIST[:], 0.0)
            s_v.inc(v.memset(STAT[:], 0.0))
        emit("vector", fv)

    def prologue_ffn_weights():
        specs = [(d_wg, d_wgT, 32, D, 8, 0),
                 (d_wu, d_wuT, 32, D, 8, 1),
                 (d_wd, d_wdT, 8, FI, 32, 2)]
        pa = [Sem(es, nc, "pa0"), Sem(es, nc, "pa1")]
        ts = [Sem(es, nc, "ts0"), Sem(es, nc, "ts1")]
        _dbg_ts.extend(ts)
        with (nc.sbuf_tensor(uname("PTILE"), [128, 2 * FI], F32) as PTILE,
              nc.sbuf_tensor(uname("TERN"), [128, 32 * D], BF16) as TERN,
              nc.sbuf_tensor(uname("TROW"), [128, 2 * FI], BF16) as TROW,
              nc.sbuf_tensor(uname("QP1"), [128, FI], F32) as QP1,
              nc.sbuf_tensor(uname("QPI"), [128, FI], I32) as QPI,
              nc.sbuf_tensor(uname("QP2"), [128, FI], F32) as QP2,
              nc.sbuf_tensor(uname("ACC"), [128, 4], F32) as ACC,
              nc.psum_tensor(uname("PSA"), [128, 512], F32) as PSA,
              nc.psum_tensor(uname("PTB"), [128, 1024], BF16) as PTB):
            for d_in, d_out, ntile, tcols, orows, widx in specs:
                def pslot(i):
                    return PTILE[:, (i % 2) * tcols:(i % 2) * tcols + tcols]

                v_emit = s_v.n
                paw = [(pa[i % 2].h, pa[i % 2].n + 16 * (i // 2 + 1))
                       for i in range(ntile)]

                def fload(g, d_in=d_in, ntile=ntile, tcols=tcols,
                          v_emit=v_emit):
                    for i in range(ntile):
                        if i < 2:
                            g.wait_ge(s_v.h, v_emit)
                        else:
                            g.wait_ge(s_v.h, v_emit + 2 * (i - 2) + 1)
                        pa[i % 2].inc(g.dma_start(
                            out=PTILE[:, (i % 2) * tcols:(i % 2) * tcols + tcols],
                            in_=d_in[i * 128:(i + 1) * 128, :]), 16)
                emit("sync", fload)

                def fred(v, ntile=ntile, tcols=tcols, paw=paw):
                    for i in range(ntile):
                        v.wait_ge(*paw[i])
                        vv(v, v.reduce_sum(
                            ACC[:, 1:2],
                            PTILE[:, (i % 2) * tcols:(i % 2) * tcols + tcols],
                            axis=AX, apply_absolute_value=True))
                        if i == 0:
                            vv(v, v.tensor_copy(ACC[:, 0:1], ACC[:, 1:2]))
                        else:
                            vv(v, v.tensor_tensor(ACC[:, 0:1], ACC[:, 0:1],
                                                  ACC[:, 1:2], OP.add))
                emit("vector", fred)

                def fmm(t):
                    t.wait_ge(s_v.h, s_v.n)
                    s_mm.inc(t.matmul(PSA[0:1, 0:1], ACC[:, 0:1], ONECOL[:],
                                      start=True, stop=True))
                emit("tensor", fmm)

                nelem = float(ntile * 128 * tcols)
                mm_now = s_mm.n

                def ftiny(v, nelem=nelem, mm_now=mm_now):
                    v.wait_ge(s_mm.h, mm_now)
                    vv(v, v.tensor_scalar(ACC[0:1, 2:3], PSA[0:1, 0:1],
                                          1.0 / nelem, 1e-5, OP.mult, OP.max))
                    vv(v, v.reciprocal(ACC[0:1, 3:4], ACC[0:1, 2:3]))
                emit("vector", ftiny)

                def fbc(t):
                    t.wait_ge(s_v.h, s_v.n)
                    s_mm.inc(t.matmul(PSA[:, 1:2], ONEROW[:], ACC[0:1, 3:4],
                                      start=True, stop=True))
                emit("tensor", fbc)

                mm_now2 = s_mm.n

                def ftiny2(v, widx=widx, mm_now2=mm_now2):
                    v.wait_ge(s_mm.h, mm_now2)
                    vv(v, v.tensor_copy(SCW[:, 3 + widx:4 + widx], PSA[:, 1:2]))
                    vv(v, v.reciprocal(SCW[:, widx:widx + 1], PSA[:, 1:2]))
                emit("vector", ftiny2)

                v_emit2 = s_v.n
                mm_guard = s_mm.n
                paw2 = [(pa[i % 2].h, pa[i % 2].n + 16 * (i // 2 + 1))
                        for i in range(ntile)]

                def fload2(g, d_in=d_in, ntile=ntile, tcols=tcols,
                           v_emit2=v_emit2):
                    for i in range(ntile):
                        if i < 2:
                            g.wait_ge(s_v.h, v_emit2)
                        else:
                            g.wait_ge(s_v.h, v_emit2 + 2 * (i - 2) + 2)
                        pa[i % 2].inc(g.dma_start(
                            out=PTILE[:, (i % 2) * tcols:(i % 2) * tcols + tcols],
                            in_=d_in[i * 128:(i + 1) * 128, :]), 16)
                emit("sync", fload2)

                def fq(v, ntile=ntile, tcols=tcols, widx=widx,
                       paw2=paw2, mm_guard=mm_guard):
                    v.wait_ge(s_mm.h, mm_guard)
                    v.wait_ge(s_v.h, s_v.n)
                    for i in range(ntile):
                        v.wait_ge(*paw2[i])
                        src = PTILE[:, (i % 2) * tcols:(i % 2) * tcols + tcols]
                        v.tensor_scalar(QP1[:, :tcols], src,
                                        SCW[:, 3 + widx:4 + widx], 0.5,
                                        OP.mult, OP.add)
                        v.tensor_copy(QPI[:, :tcols], QP1[:, :tcols])
                        v.tensor_tensor(QP1[:, :tcols], QPI[:, :tcols],
                                        QP1[:, :tcols], OP.is_gt)
                        s_v.inc(v.tensor_tensor(QP2[:, :tcols], QPI[:, :tcols],
                                                QP1[:, :tcols], OP.subtract))
                        s_v.inc(v.tensor_scalar(
                            TERN[:, i * tcols:i * tcols + tcols],
                            QP2[:, :tcols], -1.0, 1.0, OP.max, OP.min))
                emit("vector", fq)

                if dbg == "tern" and widx == 0:
                    vfin0 = s_v.n

                    def dtap(v):
                        v.wait_ge(s_v.h, vfin0)
                        v.tensor_copy(QP1[:, 0:4096], TERN[:, 0:4096])
                        s_v.inc(v.tensor_copy(QP2[:, 0:4096], TERN[:, 4096:8192]))
                    emit("vector", dtap)
                    vfin = s_v.n

                    def dst(g):
                        g.wait_ge(s_v.h, vfin)
                        s_st.inc(g.dma_start(out=d_dbg[:, 0:4096], in_=QP1[:, 0:4096]), 16)
                        s_st.inc(g.dma_start(out=d_dbg[:, 4096:8192], in_=QP2[:, 0:4096]), 16)
                    emit("gpsimd", dst)

                st_marks = {}
                for j in range(orows):
                    vb = s_v.n
                    mmb = s_mm.n

                    scb_tr = s_sc.n

                    def ftr(t, j=j, ntile=ntile, tcols=tcols, vb=vb,
                            scb_tr=scb_tr):
                        t.wait_ge(s_v.h, vb)
                        t.wait_ge(s_sc.h, scb_tr)
                        for i in range(ntile):
                            if i >= 8:
                                t.wait_ge(s_sc.h, scb_tr + (i // 8))
                            s_mm.inc(t.transpose(
                                PTB[:, (i % 8) * 128:(i % 8) * 128 + 128],
                                TERN[:, i * tcols + j * 128:
                                     i * tcols + j * 128 + 128],
                                IDB[:]))
                    emit("tensor", ftr)

                    trow_half = (j % 2) * FI

                    tsnow = ts[j % 2].n

                    def fev(s, j=j, ntile=ntile, mmb=mmb, trow_half=trow_half,
                            st_marks=st_marks, tsnow=tsnow):
                        if j >= 2 and (j - 2) in st_marks:
                            s.wait_ge(*st_marks[j - 2])
                        elif j < 2:
                            s.wait_ge(ts[j % 2].h, tsnow)
                        ng = (ntile + 7) // 8
                        for gi in range(ng):
                            cnt = min(8, ntile - gi * 8)
                            s.wait_ge(s_mm.h, mmb + gi * 8 + cnt)
                            s_sc.inc(s.copy(
                                TROW[:, trow_half + gi * 1024:
                                     trow_half + gi * 1024 + cnt * 128],
                                PTB[:, :cnt * 128]))
                    emit("scalar", fev)

                    scnow = s_sc.n

                    def fst(g, j=j, d_out=d_out, ntile=ntile, scnow=scnow,
                            trow_half=trow_half):
                        g.wait_ge(s_sc.h, scnow)
                        ts[j % 2].inc(g.dma_start(
                            out=d_out[j * 128:(j + 1) * 128, :],
                            in_=TROW[:, trow_half:trow_half + ntile * 128]), 16)
                    emit("gpsimd", fst)
                    st_marks[j] = (ts[j % 2].h, ts[j % 2].n)

            v_emit3 = s_v.n
            paw3 = [(pa[i % 2].h, pa[i % 2].n + 16 * (i // 2 + 1))
                    for i in range(8)]

            def wload3(g, v_emit3=v_emit3):
                for i in range(8):
                    if i < 2:
                        g.wait_ge(s_v.h, v_emit3)
                    else:
                        g.wait_ge(s_v.h, v_emit3 + (i - 2) + 1)
                    pa[i % 2].inc(g.dma_start(
                        out=PTILE[:, (i % 2) * FI:(i % 2) * FI + FI],
                        in_=d_win[i * 128:(i + 1) * 128, :]), 16)
            emit("sync", wload3)

            st3 = {}
            for i in range(8):
                ts0n, ts1n = ts[0].n, ts[1].n
                srcw = PTILE[:, (i % 2) * FI:(i % 2) * FI + FI]

                def wsplitS(s, i=i, ts0n=ts0n, srcw=srcw, v_emit3=v_emit3):
                    s.wait_ge(*paw3[i])
                    if i >= 1:
                        s.wait_ge(*st3[i - 1][0])
                        s.wait_ge(s_v.h, v_emit3 + i)
                    else:
                        s.wait_ge(ts[0].h, ts0n)
                        s.wait_ge(s_v.h, v_emit3)
                    s_sc.inc(s.copy(TROW[:, 0:FI], srcw))
                    s_sc.inc(s.copy(QP1[:, :FI], TROW[:, 0:FI]))
                emit("scalar", wsplitS)

                scn3 = s_sc.n

                def wsplitV(v, i=i, ts1n=ts1n, srcw=srcw, scn3=scn3):
                    v.wait_ge(s_sc.h, scn3)
                    if i >= 1:
                        v.wait_ge(*st3[i - 1][1])
                    else:
                        v.wait_ge(ts[1].h, ts1n)
                    s_v.inc(v.tensor_tensor(TROW[:, FI:2 * FI], srcw,
                                            QP1[:, :FI], OP.subtract))
                emit("vector", wsplitV)

                vnow3 = s_v.n

                def wst3(g, i=i, vnow3=vnow3):
                    g.wait_ge(s_v.h, vnow3)
                    ts[0].inc(g.dma_start(out=d_winH[i * 128:(i + 1) * 128, :],
                                          in_=TROW[:, 0:FI]), 16)
                    ts[1].inc(g.dma_start(out=d_winL[i * 128:(i + 1) * 128, :],
                                          in_=TROW[:, FI:2 * FI]), 16)
                emit("gpsimd", wst3)
                st3[i] = [(ts[0].h, ts[0].n), (ts[1].h, ts[1].n)]

    def load_x(k, first):
        stn = s_st.n

        def f(g, k=k, first=first, stn=stn):
            if not first:
                g.wait_ge(s_st.h, stn)
            for tt in range(TPC):
                s_gq.inc(g.dma_start(
                    out=X[:, tt * D:(tt + 1) * D],
                    in_=d_x[k * T + tt * 128:k * T + tt * 128 + 128, :]), 16)
        emit("gpsimd", f)
        return s_gq.n

    def store_x(k):
        vn = s_v.n

        def f(g, k=k, vn=vn):
            g.wait_ge(s_v.h, vn)
            for tt in range(TPC):
                s_st.inc(g.dma_start(
                    out=d_y[k * T + tt * 128:k * T + tt * 128 + 128, :],
                    in_=X[:, tt * D:(tt + 1) * D]), 16)
        emit("gpsimd", f)

    def ffn(k, xload_gq, wstreams):
        wg_s, wu_s, wd_s = wstreams
        with (nc.sbuf_tensor(uname("WGb"), [128, 4 * 1024], BF16) as WGb,
              nc.sbuf_tensor(uname("WUb"), [128, 4 * 1024], BF16) as WUb,
              nc.sbuf_tensor(uname("WDb"), [128, 4 * 1024], BF16) as WDb,
              nc.sbuf_tensor(uname("PROD"), [128, TPC * FI], F32) as PROD,
              nc.sbuf_tensor(uname("QA"), [128, 1024], F32) as QA,
              nc.sbuf_tensor(uname("C32"), [128, 1024], I32) as C32,
              nc.sbuf_tensor(uname("QC"), [128, 1024], F32) as QC,
              nc.sbuf_tensor(uname("XCB"), [128, 1024], F32) as XCB,
              nc.sbuf_tensor(uname("SG"), [128, TPC * 1024], F32) as SG,
              nc.sbuf_tensor(uname("N1T"), [128, 4096], BF16) as N1T,
              nc.sbuf_tensor(uname("N2T"), [128, 16384], BF16) as N2T,
              nc.sbuf_tensor(uname("NQ"), [128, 4096], BF16) as NQ):
            wg_s.buf, wu_s.buf, wd_s.buf = WGb, WUb, WDb

            with (nc.psum_tensor(uname("PTB1"), [128, 1024], BF16) as PTB1,
                  nc.psum_tensor(uname("PW1"), [128, 8], F32) as PW1):
                for tt in range(TPC):
                    xs = X[:, tt * D:(tt + 1) * D]

                    def f1v(v, tt=tt, xs=xs):
                        v.wait_ge(s_sc.h, s_sc.n)
                        if tt == 0:
                            v.wait_ge(s_gq.h, xload_gq)
                        vv(v, v.reduce_sum(scol(C_T1, tt), xs, axis=AX))
                        vv(v, v.tensor_scalar(scol(C_MEAN, tt), scol(C_T1, tt),
                                              1.0 / D, None, OP.mult))
                        s_v.inc(v.tensor_scalar(XCB[:], xs, scol(C_MEAN, tt),
                                                None, OP.subtract))
                    emit("vector", f1v)
                    a1 = s_v.n

                    def f1s(s, tt=tt):
                        s.wait_ge(s_v.h, s_v.n)
                        ss(s, s.activation(QADUM[:], XCB[:], AF.Square,
                                           accum_out=scol(C_VS, tt)))
                        ss(s, s.activation(scol(C_T1, tt), scol(C_VS, tt),
                                           AF.Ln, bias=CONSTS[:, 0:1],
                                           scale=1.0 / D))
                        s_sc.inc(s.activation(scol(C_RSTD, tt), scol(C_T1, tt),
                                              AF.Exp, bias=0.0, scale=-0.5))
                    emit("scalar", f1s)
                    c1 = s_sc.n
                    b1 = s_v.n

                    def f1v2(v, tt=tt):
                        vv(v, v.reduce_max(scol(C_AM, tt), XCB[:], axis=AX,
                                           apply_absolute_value=True))
                        v.wait_ge(s_sc.h, s_sc.n)
                        vv(v, v.tensor_scalar(scol(C_AM2, tt), scol(C_AM, tt),
                                              scol(C_RSTD, tt), 1e-5,
                                              OP.mult, OP.max))
                        vv(v, v.reciprocal(scol(C_T1, tt), scol(C_AM2, tt)))
                        vv(v, v.tensor_scalar(scol(C_S1, tt), scol(C_T1, tt),
                                              127.0, None, OP.mult))
                        vv(v, v.tensor_scalar(scol(C_Q1, tt), scol(C_S1, tt),
                                              scol(C_RSTD, tt), None, OP.mult))
                        vv(v, v.reciprocal(scol(C_D1, tt), scol(C_S1, tt)))
                        vv(v, v.tensor_scalar(scol(C_GSC, tt), scol(C_D1, tt),
                                              SCW[:, 0:1], None, OP.mult))
                        vv(v, v.tensor_scalar(scol(C_C, tt), scol(C_D1, tt),
                                              SCW[:, 1:2], None, OP.mult))
                        vv(v, v.reciprocal(scol(C_T2, tt), scol(C_C, tt)))
                        vv(v, v.tensor_scalar(scol(C_BND, tt), scol(C_T2, tt),
                                              1e-5, None, OP.mult))
                        v.wait_ge(s_mm.h, nqg["mm"])
                        for hf in range(2):
                            c0 = hf * 512
                            v.tensor_scalar(QA[:, c0:c0 + 512],
                                            XCB[:, c0:c0 + 512],
                                            scol(C_Q1, tt), 0.5,
                                            OP.mult, OP.add)
                            v.tensor_copy(C32[:, c0:c0 + 512], QA[:, c0:c0 + 512])
                            v.tensor_tensor(QA[:, c0:c0 + 512], C32[:, c0:c0 + 512],
                                            QA[:, c0:c0 + 512], OP.is_gt)
                            s_v.inc(v.tensor_tensor(
                                NQ[:, c0:c0 + 512], C32[:, c0:c0 + 512],
                                QA[:, c0:c0 + 512], OP.subtract))
                    emit("vector", f1v2)

                    mmb = s_mm.n
                    evp1 = s_ev.n
                    vfin1 = s_v.n

                    def f1t(t, tt=tt, evp1=evp1, vfin1=vfin1,
                            a1=a1, b1=b1, c1=c1):
                        pewarm(t, PW1, [(s_v.h, max(0, a1 - 4)), (s_v.h, a1),
                                        (s_sc.h, c1), (s_v.h, b1 + 5),
                                        (s_v.h, b1 + 10)])
                        t.wait_ge(s_ev.h, evp1)
                        for hf in range(2):
                            t.wait_ge(s_v.h, vfin1 - 1 + hf)
                            for f in range(4):
                                fb = hf * 4 + f
                                s_mm.inc(t.transpose(
                                    PTB1[:, fb * 128:(fb + 1) * 128],
                                    NQ[:, fb * 128:(fb + 1) * 128], IDB[:]))
                    emit("tensor", f1t)
                    nqg["mm"] = s_mm.n

                    def f1e(s, tt=tt, mmb=mmb):
                        s.wait_ge(s_mm.h, mmb + 8)
                        s_ev.inc(s.copy(
                            N1T[:, tt * 1024:(tt + 1) * 1024], PTB1[:]))
                    emit("scalar", f1e)

            with ExitStack() as pse:
                PS = [pse.enter_context(
                    nc.psum_tensor(uname(f"PSF{i}"), [128, 512], F32)) for i in range(8)]

                n1t_ev = s_ev.n
                prev = {"evg": None, "vpr": None}
                for q in range(4):
                    for wsel in ("g", "u"):
                        stream = wg_s if wsel == "g" else wu_s
                        dscr = d_wgT if wsel == "g" else d_wuT
                        tiles = [dscr[f * 128:(f + 1) * 128, q * 1024:(q + 1) * 1024]
                                 for f in range(8)]
                        tile_waits, bidx = stream_loads(stream, tiles, 8)

                        vb = s_v.n
                        mm0 = s_mm.n
                        evg, vpr = prev["evg"], prev["vpr"]

                        def fmm(t, q=q, wsel=wsel, stream=stream, bidx=bidx,
                                tile_waits=tile_waits, evg=evg, vpr=vpr, vb=vb,
                                n1t_ev=n1t_ev):
                            for f in range(8):
                                t.wait_ge(*tile_waits[f])
                                if f == 0 and wsel == "g" and q == 0:
                                    if vpr is None:
                                        t.wait_ge(s_v.h, vb)
                                    t.wait_ge(s_ev.h, n1t_ev)
                                w = stream.slot(bidx + f)
                                ins = None
                                for tt in range(TPC):
                                    lhsT = N1T[:, tt * 1024 + f * 128:
                                               tt * 1024 + f * 128 + 128]
                                    for ncn in range(2):
                                        g = tt * 2 + ncn
                                        if f == 0:
                                            if wsel == "u":
                                                t.wait_ge(s_ev.h, evg + g + 1)
                                            elif vpr is not None:
                                                t.wait_ge(s_v.h, vpr + g + 1)
                                        ins = t.matmul(
                                            PS[g][:], lhsT,
                                            w[:, ncn * 512:ncn * 512 + 512],
                                            start=(f == 0), stop=(f == 7))
                                        if f == 7:
                                            s_mm.inc(ins)
                                if f < 7:
                                    s_pec.inc(ins)
                        emit("tensor", fmm)

                        if wsel == "g":
                            prev["evg"] = s_ev.n
                            def fev(s, mm0=mm0):
                                for g in range(8):
                                    s.wait_ge(s_mm.h, mm0 + g + 1)
                                    tt, ncn = g // 2, g % 2
                                    s_ev.inc(s.activation(
                                        SG[:, tt * 1024 + ncn * 512:
                                           tt * 1024 + ncn * 512 + 512],
                                        PS[g][:], AF.Silu, bias=0.0,
                                        scale=scol(C_GSC, tt)))
                            emit("scalar", fev)
                        else:
                            prev["vpr"] = s_v.n

                            def fpr(v, q=q, mm0=mm0):
                                for g in range(8):
                                    v.wait_ge(s_mm.h, mm0 + g + 1)
                                    tt, ncn = g // 2, g % 2
                                    s_v.inc(v.tensor_tensor(
                                        PROD[:, tt * FI + q * 1024 + ncn * 512:
                                             tt * FI + q * 1024 + ncn * 512 + 512],
                                        SG[:, tt * 1024 + ncn * 512:
                                           tt * 1024 + ncn * 512 + 512],
                                        PS[g][:], OP.mult))
                                for tt in range(TPC):
                                    vv(v, v.reduce_max(
                                        STAT[:, 64 + tt * 4 + q:65 + tt * 4 + q],
                                        PROD[:, tt * FI + q * 1024:
                                             tt * FI + q * 1024 + 1024],
                                        axis=AX, apply_absolute_value=True))
                            emit("vector", fpr)

            if dbg == "prod" and k == 0:
                def dvp(v):
                    s_v.inc(v.tensor_copy(QADUM[:], PROD[:, 0:1024]))
                emit("vector", dvp)
                vfin = s_v.n

                def dstp(g):
                    g.wait_ge(s_v.h, vfin)
                    s_st.inc(g.dma_start(out=d_dbg[:, 0:1024], in_=QADUM[:]), 16)
                    g.wait_ge(s_st.h, s_st.n)
                emit("gpsimd", dstp)

            with nc.psum_tensor(uname("PTB2"), [128, 1024], BF16) as PTB2:
                for tt in range(TPC):
                    pre3 = s_v.n

                    def f3v(v, tt=tt):
                        pr = PROD[:, tt * FI:(tt + 1) * FI]
                        vv(v, v.reduce_max(scol(C_T1, tt),
                                           STAT[:, 64 + tt * 4:64 + tt * 4 + 4],
                                           axis=AX, apply_absolute_value=False))
                        vv(v, v.tensor_tensor(scol(C_AM2, tt), scol(C_T1, tt),
                                              scol(C_BND, tt), OP.max))
                        vv(v, v.reciprocal(scol(C_T2, tt), scol(C_AM2, tt)))
                        vv(v, v.tensor_scalar(scol(C_S2, tt), scol(C_T2, tt),
                                              127.0, None, OP.mult))
                        vv(v, v.reciprocal(scol(C_T1, tt), scol(C_S2, tt)))
                        vv(v, v.tensor_scalar(scol(C_T2, tt), scol(C_T1, tt),
                                              scol(C_C, tt), None, OP.mult))
                        vv(v, v.tensor_scalar(scol(C_C2, tt), scol(C_T2, tt),
                                              SCW[:, 2:3], 0.5, OP.mult, OP.mult))
                        v.wait_ge(s_mm.h, nqg["mm"])
                        for sl in range(4):
                            p = pr[:, sl * 1024:(sl + 1) * 1024]
                            v.tensor_scalar(QA[:], p, scol(C_S2, tt), 0.5,
                                            OP.mult, OP.add)
                            v.tensor_copy(C32[:], QA[:])
                            v.tensor_tensor(QA[:], C32[:], QA[:], OP.is_gt)
                            s_v.inc(v.tensor_tensor(
                                NQ[:, sl * 1024:(sl + 1) * 1024],
                                C32[:], QA[:], OP.subtract))
                    emit("vector", f3v)

                    mmb = s_mm.n
                    evb2 = s_ev.n

                    def f3t(t, tt=tt, pre3=pre3, evb2=evb2):
                        t.wait_ge(s_ev.h, evb2)
                        for blk in range(4):
                            t.wait_ge(s_v.h, pre3 + 8 + blk)
                            if blk > 0:
                                t.wait_ge(s_ev.h, evb2 + blk)
                            for f in range(8):
                                s_mm.inc(t.transpose(
                                    PTB2[:, f * 128:(f + 1) * 128],
                                    NQ[:, blk * 1024 + f * 128:
                                       blk * 1024 + f * 128 + 128], IDB[:]))
                    emit("tensor", f3t)
                    nqg["mm"] = s_mm.n

                    def f3e(s, tt=tt, mmb=mmb):
                        for blk in range(4):
                            s.wait_ge(s_mm.h, mmb + blk * 8 + 8)
                            s_ev.inc(s.copy(
                                N2T[:, tt * 4096 + blk * 1024:
                                    tt * 4096 + blk * 1024 + 1024], PTB2[:]))
                    emit("scalar", f3e)

            n2t_ready = s_v.n
            n2t_ev = s_ev.n
            with ExitStack() as pse:
                PS = [pse.enter_context(
                    nc.psum_tensor(uname(f"PSD{i}"), [128, 512], F32)) for i in range(8)]

                tiles = [d_wdT[f2 * 128:(f2 + 1) * 128, :] for f2 in range(32)]
                tile_waits, bidx = stream_loads(wd_s, tiles, 8)

                def fmm(t, tile_waits=tile_waits, bidx=bidx,
                        n2t_ready=n2t_ready, n2t_ev=n2t_ev):
                    t.wait_ge(s_v.h, n2t_ready)
                    t.wait_ge(s_ev.h, n2t_ev)
                    for f2 in range(32):
                        t.wait_ge(*tile_waits[f2])
                        w = wd_s.slot(bidx + f2)
                        ins = None
                        for tt in range(TPC):
                            lhsT = N2T[:, tt * 4096 + f2 * 128:
                                       tt * 4096 + f2 * 128 + 128]
                            for ncn in range(2):
                                ins = t.matmul(
                                    PS[tt * 2 + ncn][:], lhsT,
                                    w[:, ncn * 512:ncn * 512 + 512],
                                    start=(f2 == 0), stop=(f2 == 31))
                                if f2 == 31:
                                    s_mm.inc(ins)
                        if f2 < 31:
                            s_pec.inc(ins)
                emit("tensor", fmm)

                mm0 = s_mm.n - 8

                def fres(v, mm0=mm0):
                    for g in range(8):
                        v.wait_ge(s_mm.h, mm0 + g + 1)
                        tt, ncn = g // 2, g % 2
                        xs = X[:, tt * D + ncn * 512:tt * D + ncn * 512 + 512]
                        s_v.inc(v.scalar_tensor_tensor(
                            xs, PS[g][:], scol(C_C2, tt), xs, OP.mult, OP.add))
                emit("vector", fres)

    def mamba(k, wstreams, upto="m4"):
        wi_s, wo_s = wstreams
        with (nc.sbuf_tensor(uname("WIb"), [128, 4 * 1024], BF16) as WIb,
              nc.sbuf_tensor(uname("WOb"), [128, 2 * 1024], F32) as WOb,
              nc.sbuf_tensor(uname("XCT"), [128, 16 * 515], F32) as XCT,
              nc.sbuf_tensor(uname("SZ"), [128, 16 * 512], F32) as SZ,
              nc.sbuf_tensor(uname("U"), [128, 16 * 512], F32) as U,
              nc.sbuf_tensor(uname("YS"), [128, 16 * 512], F32) as YS,
              nc.sbuf_tensor(uname("XLTH"), [128, 8 * 512], BF16) as XLTH,
              nc.sbuf_tensor(uname("XLTL"), [128, 8 * 512], BF16) as XLTL,
              nc.sbuf_tensor(uname("UBs"), [128, 2 * 512], F32) as UBs,
              nc.sbuf_tensor(uname("XPS"), [48, 512], F32) as XPS,
              nc.sbuf_tensor(uname("TM"), [128, 2 * 512], F32) as TM,
              nc.sbuf_tensor(uname("XLN"), [128, 1024], F32) as XLN):
            wi_s.buf, wo_s.buf = WIb, WOb
            EXPS = QADUM

            def xct(ci):
                return XCT[:, ci * 515:ci * 515 + 515]

            with (nc.psum_tensor(uname("PTF"), [128, 1024], F32) as PTF,
                  nc.psum_tensor(uname("PW4"), [128, 8], F32) as PW4):
                for tt in range(TPC):
                    xs = X[:, tt * D:(tt + 1) * D]

                    def m0v(v, tt=tt, xs=xs):
                        v.wait_ge(s_sc.h, s_sc.n)
                        v.wait_ge(s_mm.h, xlng["mm"])
                        vv(v, v.reduce_sum(scol(C_T1, tt), xs, axis=AX))
                        vv(v, v.tensor_scalar(scol(C_MEAN, tt), scol(C_T1, tt),
                                              1.0 / D, None, OP.mult))
                        s_v.inc(v.tensor_scalar(XLN[:], xs, scol(C_MEAN, tt),
                                                None, OP.subtract))
                    emit("vector", m0v)
                    a2 = s_v.n

                    def m0s(s, tt=tt):
                        s.wait_ge(s_v.h, s_v.n)
                        ss(s, s.activation(QADUM[:], XLN[:], AF.Square,
                                           accum_out=scol(C_VS, tt)))
                        ss(s, s.activation(scol(C_T1, tt), scol(C_VS, tt),
                                           AF.Ln, bias=CONSTS[:, 0:1],
                                           scale=1.0 / D))
                        s_sc.inc(s.activation(scol(C_RSTD, tt), scol(C_T1, tt),
                                              AF.Exp, bias=0.0, scale=-0.5))
                    emit("scalar", m0s)
                    c2m = s_sc.n

                    def m0v2(v, tt=tt):
                        v.wait_ge(s_sc.h, s_sc.n)
                        s_v.inc(v.tensor_scalar(XLN[:], XLN[:],
                                                scol(C_RSTD, tt), None, OP.mult))
                    emit("vector", m0v2)

                    mmb = s_mm.n
                    evp = s_ev.n

                    def m0t(t, tt=tt, evp=evp, a2=a2, c2m=c2m, vfin=s_v.n):
                        pewarm(t, PW4, [(s_v.h, max(0, a2 - 6)),
                                        (s_v.h, max(0, a2 - 3)), (s_v.h, a2),
                                        (s_sc.h, c2m)])
                        t.wait_ge(s_v.h, vfin)
                        t.wait_ge(s_ev.h, evp)
                        for f in range(8):
                            s_mm.inc(t.transpose(PTF[:, f * 128:(f + 1) * 128],
                                                 XLN[:, f * 128:(f + 1) * 128],
                                                 IDF[:]))
                    emit("tensor", m0t)
                    xlng["mm"] = s_mm.n

                    def m0e(s, tt=tt, mmb=mmb):
                        s.wait_ge(s_mm.h, mmb + 8)
                        dst = XLTH.rearrange("p (f c) -> p f c", c=512)[
                            :, :, tt * 128:tt * 128 + 128]
                        src = PTF.rearrange("p (f c) -> p f c", c=128)
                        s_ev.inc(s.copy(dst, src))
                    emit("scalar", m0e)

                    evh = s_ev.n

                    def m0l(v, tt=tt, evh=evh):
                        v.wait_ge(s_ev.h, evh)
                        dstH = XLTH.rearrange("p (f c) -> p f c", c=512)[
                            :, :, tt * 128:tt * 128 + 128]
                        dstL = XLTL.rearrange("p (f c) -> p f c", c=512)[
                            :, :, tt * 128:tt * 128 + 128]
                        qv = QADUM.rearrange("p (f c) -> p f c", c=128)
                        src = PTF.rearrange("p (f c) -> p f c", c=128)
                        s_v.inc(v.tensor_copy(qv, dstH))
                        s_v.inc(v.tensor_tensor(dstL, src, qv, OP.subtract))
                    emit("vector", m0l)

            if upto == "m0":
                return
            xc_ev_marks = {}
            xlt_ready = s_v.n
            xlt_ev = s_ev.n
            with ExitStack() as pse:
                PS = [pse.enter_context(
                    nc.psum_tensor(uname(f"PSM{i}"), [128, 512], F32)) for i in range(8)]

                for mg in range(4):
                    tiles = []
                    for f in range(8):
                        tiles.append(d_winH[f * 128:(f + 1) * 128,
                                            mg * 1024:(mg + 1) * 1024])
                        tiles.append(d_winL[f * 128:(f + 1) * 128,
                                            mg * 1024:(mg + 1) * 1024])
                    tile_waits, bidx = stream_loads(wi_s, tiles, 8)
                    evb = s_ev.n

                    def fmm(t, mg=mg, tile_waits=tile_waits, bidx=bidx, evb=evb,
                            xlt_ready=xlt_ready, xlt_ev=xlt_ev):
                        if mg == 0:
                            t.wait_ge(s_v.h, xlt_ready)
                            t.wait_ge(s_ev.h, xlt_ev)
                        else:
                            t.wait_ge(s_ev.h, evb)
                        for f in range(8):
                            t.wait_ge(*tile_waits[2 * f])
                            t.wait_ge(*tile_waits[2 * f + 1])
                            wh = wi_s.slot(bidx + 2 * f)
                            wl = wi_s.slot(bidx + 2 * f + 1)
                            rh = XLTH[:, f * 512:(f + 1) * 512]
                            rl = XLTL[:, f * 512:(f + 1) * 512]
                            for m8 in range(8):
                                whc = wh[:, m8 * 128:m8 * 128 + 128]
                                wlc = wl[:, m8 * 128:m8 * 128 + 128]
                                t.matmul(PS[m8][:], whc, rh,
                                         start=(f == 0), stop=False)
                                i2 = t.matmul(PS[m8][:], whc, rl,
                                              start=False, stop=False)
                                i3 = t.matmul(PS[m8][:], wlc, rh,
                                              start=False, stop=(f == 7))
                                if f == 7:
                                    s_mm.inc(i3)
                                elif m8 == 7:
                                    s_pec.inc(i3)
                                if m8 == 7:
                                    s_pec.inc(i2)
                    emit("tensor", fmm)

                    mm0 = s_mm.n - 8

                    def fev(s, mg=mg, mm0=mm0):
                        for m8 in range(8):
                            m = mg * 8 + m8
                            s.wait_ge(s_mm.h, mm0 + m8 + 1)
                            if m < 16:
                                s_ev.inc(s.copy(xct(m)[:, 3:515], PS[m8][:]))
                            else:
                                s_ev.inc(s.activation(
                                    SZ[:, (m - 16) * 512:(m - 15) * 512],
                                    PS[m8][:], AF.Silu))
                            if m < 16:
                                xc_ev_marks[m] = s_ev.n
                    emit("scalar", fev)

            if upto == "m1":
                return
            ev_all = s_ev.n
            v_conv0 = s_v.n
            sc_silu0 = s_sc.n

            def m2v(v, ev_all=ev_all, sc_silu0=sc_silu0):
                for ci in range(16):
                    v.wait_ge(s_ev.h, xc_ev_marks[ci])
                    if ci >= 2:
                        v.wait_ge(s_sc.h, sc_silu0 + (ci - 2) + 1)
                    xc = xct(ci)
                    e = EXPS[:, (ci % 2) * 512:(ci % 2) * 512 + 512]
                    vv(v, v.tensor_copy(xc[:, 0:3], HIST[:, ci * 3:ci * 3 + 3]))
                    cw = CONVW[:, ci * 4:ci * 4 + 4]
                    v.tensor_scalar(e, xc[:, 3:515], cw[:, 3:4], None, OP.mult)
                    v.scalar_tensor_tensor(e, xc[:, 2:514], cw[:, 2:3], e,
                                           OP.mult, OP.add)
                    v.scalar_tensor_tensor(e, xc[:, 1:513], cw[:, 1:2], e,
                                           OP.mult, OP.add)
                    s_v.inc(v.scalar_tensor_tensor(e, xc[:, 0:512], cw[:, 0:1],
                                                   e, OP.mult, OP.add))
                    v.tensor_copy(HIST[:, ci * 3:ci * 3 + 3], xc[:, 512:515])
            emit("vector", m2v)

            def m2s(s, v_conv0=v_conv0):
                for ci in range(16):
                    s.wait_ge(s_v.h, v_conv0 + 2 * ci + 2)
                    e = EXPS[:, (ci % 2) * 512:(ci % 2) * 512 + 512]
                    s_sc.inc(s.activation(U[:, ci * 512:(ci + 1) * 512],
                                          e, AF.Silu))
            emit("scalar", m2s)

            with nc.psum_tensor(uname("PXP"), [48, 512], F32) as PXP:
                sc_after = s_sc.n

                def m2t(t, sc_after=sc_after):
                    for ci in range(16):
                        t.wait_ge(s_sc.h, sc_after - (15 - ci))
                        s_mm.inc(t.matmul(PXP[0:16, :],
                                          WXS[:, ci * 32:ci * 32 + 16],
                                          U[:, ci * 512:(ci + 1) * 512],
                                          start=(ci == 0), stop=(ci == 15)))
                        s_mm.inc(t.matmul(PXP[32:48, :],
                                          WXS[:, ci * 32 + 16:ci * 32 + 32],
                                          U[:, ci * 512:(ci + 1) * 512],
                                          start=(ci == 0), stop=(ci == 15)))
                emit("tensor", m2t)

                mm_now = s_mm.n

                def m2e(v, mm_now=mm_now):
                    v.wait_ge(s_mm.h, mm_now)
                    vv(v, v.tensor_copy(XPS[0:16, :], PXP[0:16, :]))
                    vv(v, v.tensor_copy(XPS[32:48, :], PXP[32:48, :]))
                emit("vector", m2e)

            if upto == "m2":
                return
            pse_w = ExitStack()
            PSA4 = [pse_w.enter_context(
                nc.psum_tensor(uname(f"PSW{i}"), [128, 512], F32))
                for i in range(4)]
            with (nc.psum_tensor(uname("PDT"), [128, 512], F32) as PDT,
                  nc.psum_tensor(uname("PBP"), [128, 512], F32) as PBP):
                tiles = [d_wout[ci * 128:(ci + 1) * 128, :] for ci in range(16)]
                wo_waits, wo_bidx = stream_loads(wo_s, tiles, 4,
                                                 mm_final=s_mm.n + 36)
                y_marks = {}
                xp_ready = s_v.n
                mm0s = {}
                vpre = {}

                def emit_m3t(ci):
                    mm0s[ci] = s_mm.n
                    if ci == 0:
                        waits = [(s_v.h, xp_ready)]
                    else:
                        waits = [(s_sc.h, s_sc.n), (s_v.h, vpre[ci - 1] + 1)]

                    def m3t(t, ci=ci, waits=waits):
                        for h, nv in waits:
                            t.wait_ge(h, nv)
                        s_mm.inc(t.matmul(PDT[:],
                                          WDTB[0:16, ci * 128:ci * 128 + 128],
                                          XPS[0:16, :], start=True, stop=True))
                        s_mm.inc(t.matmul(PBP[:],
                                          WDTB[32:48, ci * 128:ci * 128 + 128],
                                          XPS[32:48, :], start=True, stop=True))
                    emit("tensor", m3t)

                emit_m3t(0)
                for ci in range(16):
                    mm0 = mm0s[ci]

                    def m3s(s, ci=ci, mm0=mm0):
                        s.wait_ge(s_mm.h, mm0 + 1)
                        s.activation(EXPS[:, 0:512], PDT[:], AF.Exp,
                                     bias=BDT[:, ci:ci + 1])
                        s_sc.inc(s.activation(xct(ci)[:, 3:515], EXPS[:, 0:512],
                                              AF.Ln, bias=1.0))
                    emit("scalar", m3s)

                    vpre[ci] = s_v.n

                    def m3v(v, ci=ci, mm0=mm0):
                        v.wait_ge(s_sc.h, s_sc.n)
                        a = xct(ci)[:, 3:515]
                        v.tensor_scalar(a, a, 1e-4, 1.0, OP.max, OP.min)
                        v.tensor_scalar(a, a, -1.0, 1.0, OP.mult, OP.add)
                        v.wait_ge(s_mm.h, mm0 + 2)
                        ub = UBs[:, (ci % 2) * 512:(ci % 2) * 512 + 512]
                        s_v.inc(v.tensor_tensor(
                            ub, U[:, ci * 512:(ci + 1) * 512], PBP[:], OP.mult))
                        v.tensor_tensor_scan(YS[:, ci * 512:(ci + 1) * 512],
                                             a, ub, H[:, ci:ci + 1],
                                             OP.mult, OP.add)
                        v.scalar_tensor_tensor(U[:, ci * 512:(ci + 1) * 512],
                                               U[:, ci * 512:(ci + 1) * 512],
                                               DPAR[:, ci:ci + 1],
                                               YS[:, ci * 512:(ci + 1) * 512],
                                               OP.mult, OP.add)
                        s_v.inc(v.tensor_tensor(SZ[:, ci * 512:(ci + 1) * 512],
                                                U[:, ci * 512:(ci + 1) * 512],
                                                SZ[:, ci * 512:(ci + 1) * 512],
                                                OP.mult))
                        vv(v, v.tensor_copy(H[:, ci:ci + 1],
                                            YS[:, ci * 512 + 511:ci * 512 + 512]))
                    emit("vector", m3v)
                    y_marks[ci] = s_v.n
                    if ci < 15:
                        emit_m3t(ci + 1)

                    def m3w(t, ci=ci):
                        t.wait_ge(*wo_waits[ci])
                        t.wait_ge(s_v.h, y_marks[ci])
                        w = wo_s.slot(wo_bidx + ci)
                        ins = None
                        for mo in range(4):
                            ins = t.matmul(PSA4[mo][:],
                                           w[:, mo * 128:mo * 128 + 128],
                                           SZ[:, ci * 512:(ci + 1) * 512],
                                           start=(ci == 0), stop=(ci == 15))
                            if ci == 15:
                                s_mm.inc(ins)
                        if ci < 15:
                            s_pec.inc(ins)
                    emit("tensor", m3w)

            if upto == "m3":
                return
            mm_half1 = s_mm.n
            y_ready = s_v.n
            with pse_w:
                PS4b = [pse_w.enter_context(
                    nc.psum_tensor(uname(f"PSO{i}"), [128, 512], F32))
                    for i in range(4)]
                PS = PSA4 + PS4b
                tiles = [d_wout[ci * 128:(ci + 1) * 128, :] for ci in range(16)]
                tile_waits, bidx = stream_loads(wo_s, tiles, 4)

                def fmm(t, tile_waits=tile_waits, bidx=bidx, y_ready=y_ready):
                    t.wait_ge(s_v.h, y_ready)
                    for ci in range(16):
                        t.wait_ge(*tile_waits[ci])
                        w = wo_s.slot(bidx + ci)
                        ins = None
                        for mo in range(4, 8):
                            ins = t.matmul(PS[mo][:],
                                           w[:, mo * 128:mo * 128 + 128],
                                           SZ[:, ci * 512:(ci + 1) * 512],
                                           start=(ci == 0), stop=(ci == 15))
                            if ci == 15:
                                s_mm.inc(ins)
                        if ci < 15:
                            s_pec.inc(ins)
                emit("tensor", fmm)
                mm_half2 = s_mm.n

                tr_done = {}
                for mo in range(8):
                    if upto == "m4a":
                        def m4x(s, mo=mo, mm0=mm0):
                            s.wait_ge(s_mm.h, mm0 + mo + 1)
                            s_ev.inc(s.copy(TM[:, 0:512], PS[mo][:]))
                        emit("scalar", m4x)
                        continue
                    evb = s_ev.n
                    tm = TM[:, (mo % 2) * 512:(mo % 2) * 512 + 512]

                    mmw = mm_half1 if mo < 4 else (mm_half2 - 4 + (mo - 4) + 1)

                    def m4s(s, mo=mo, mmw=mmw, tm=tm, trw=tr_done.get(mo - 2)):
                        s.wait_ge(s_mm.h, mmw)
                        if trw is not None:
                            s.wait_ge(s_mm.h, trw)
                        s_ev.inc(s.copy(tm, PS[mo][:]))
                    emit("scalar", m4s)

                    mmt = s_mm.n

                    def m4t(t, mo=mo, evb=evb, tm=tm):
                        t.wait_ge(s_ev.h, evb + 1)
                        for tt in range(TPC):
                            s_mm.inc(t.transpose(
                                PS[mo][:, tt * 128:tt * 128 + 128],
                                tm[:, tt * 128:tt * 128 + 128], IDF[:]))
                    emit("tensor", m4t)
                    tr_done[mo] = s_mm.n

                    def m4v(v, mo=mo, mmt=mmt):
                        v.wait_ge(s_mm.h, mmt + TPC)
                        ins = None
                        for tt in range(TPC):
                            xs = X[:, tt * D + mo * 128:tt * D + mo * 128 + 128]
                            ins = v.scalar_tensor_tensor(
                                xs, PS[mo][:, tt * 128:tt * 128 + 128],
                                0.0, xs, OP.bypass, OP.add)
                        s_v.inc(ins)
                    emit("vector", m4v)

    prologue_smalls()
    prologue_ffn_weights()
    scratch_ready = [(s.h, s.n) for s in _dbg_ts]

    if dbg == "w":
        with (nc.sbuf_tensor(uname("DBB"), [128, 4096], BF16) as DBB,
              nc.sbuf_tensor(uname("DBF"), [128, 8192], F32) as DBF):
            sq0 = s_sq.n
            tsv = [(s.h, s.n) for s in _dbg_ts]

            def dl(q):
                for h, nv in tsv:
                    q.wait_ge(h, nv)
                s_sq.inc(q.dma_start(out=DBB[:], in_=d_wgT[128:256, :]), 16)
            emit("sync", dl)

            def dv(v):
                v.wait_ge(s_sq.h, sq0 + 16)
                v.tensor_copy(DBF[:, 0:4096], DBB[:])
                v.tensor_copy(DBF[:, 4096:4100], SCW[:, 0:4])
                s_v.inc(v.tensor_copy(DBF[:, 4100:4104], SCW[:, 4:8]))
            emit("vector", dv)

            def dl2(q):
                s_sq.inc(q.dma_start(out=DBB[:, 0:1024], in_=d_wdT[0:128, 0:1024]), 16)
            emit("sync", dl2)
            sq1 = s_sq.n

            def dv2(v):
                v.wait_ge(s_sq.h, sq1)
                s_v.inc(v.tensor_copy(DBF[:, 4608:5632], DBB[:, 0:1024]))
            emit("vector", dv2)
            vfin = s_v.n

            def dst(g):
                g.wait_ge(s_v.h, vfin)
                s_st.inc(g.dma_start(out=d_dbg[:, 0:8192], in_=DBF[:]), 16)
                g.wait_ge(s_st.h, s_st.n)
            emit("gpsimd", dst)
        es.close()
        return nc

    wg_s = WStream(es, nc, "wg", None, 4, 1024)
    wu_s = WStream(es, nc, "wu", None, 4, 1024)
    wd_s = WStream(es, nc, "wd", None, 4, 1024)
    wi_s = WStream(es, nc, "wi", None, 4, 1024)
    wo_s = WStream(es, nc, "wo", None, 2, 1024)
    wg_s.first_waits = scratch_ready
    wu_s.first_waits = scratch_ready
    wd_s.first_waits = scratch_ready
    wi_s.first_waits = scratch_ready

    for k in range(NCH):
        xload_gq = load_x(k, first=(k == 0))
        if stage in ("full", "ffn1", "mamba"):
            ffn(k, xload_gq, (wg_s, wu_s, wd_s))
        if stage in ("full", "mamba") or stage.startswith("m"):
            mamba(k, (wi_s, wo_s),
                  upto=stage if stage.startswith("m") and stage != "mamba" else "m4")
        if stage == "full":
            ffn(k, xload_gq, (wg_s, wu_s, wd_s))
        store_x(k)

    def fin(g):
        g.wait_ge(s_st.h, s_st.n)
    emit("gpsimd", fin)

    es.close()
    return nc


def _prep_inputs(L0_attn_x, L0_attn_W_in, L0_attn_conv_w, L0_attn_W_x,
                 L0_attn_W_dt, L0_attn_b_dt, L0_attn_W_Bp, L0_attn_D_param,
                 L0_attn_W_out, L0_ffn_w_gate, L0_ffn_w_up, L0_ffn_w_down):
    import ml_dtypes
    f32 = np.float32
    x = np.ascontiguousarray(np.asarray(L0_attn_x, f32))
    w_dtb = np.zeros((48, DI), f32)
    w_dtb[0:16] = np.asarray(L0_attn_W_dt, f32)
    w_dtb[32:48] = np.asarray(L0_attn_W_Bp, f32)
    shared = dict(
        w_in=np.ascontiguousarray(np.asarray(L0_attn_W_in, f32)),
        w_x=np.ascontiguousarray(np.asarray(L0_attn_W_x, f32)),
        w_dtb=w_dtb,
        b_dt_col=np.ascontiguousarray(
            np.asarray(L0_attn_b_dt, f32).reshape(16, 128).T),
        d_col=np.ascontiguousarray(
            np.asarray(L0_attn_D_param, f32).reshape(16, 128).T),
        conv_col=np.ascontiguousarray(
            np.asarray(L0_attn_conv_w, f32).reshape(16, 128, 4)
            .transpose(1, 0, 2).reshape(128, 64)),
        w_out=np.ascontiguousarray(np.asarray(L0_attn_W_out, f32)),
        w_gate=np.ascontiguousarray(np.asarray(L0_ffn_w_gate, f32)),
        w_up=np.ascontiguousarray(np.asarray(L0_ffn_w_up, f32)),
        w_down=np.ascontiguousarray(np.asarray(L0_ffn_w_down, f32)),
        idf=np.eye(128, dtype=f32),
        idb=np.eye(128, dtype=ml_dtypes.bfloat16),
        onerow=np.ones((1, 128), f32),
        onecol=np.ones((128, 1), f32),
        consts=np.tile(np.array([[1e-5, 0.0, 0.0, 0.0]], f32), (128, 1)),
    )
    return x, shared


def kernel(**inputs):
    x, shared = _prep_inputs(**inputs)
    nc = build_program("full")
    in_maps = [dict(x_in=np.ascontiguousarray(x[c]), **shared) for c in range(8)]
    res = run_bass_kernel_spmd(nc, in_maps, core_ids=list(range(8)))
    return np.stack([np.asarray(r["y_out"], np.float32) for r in res.results],
                    axis=0)


# version 12
# speedup vs baseline: 1.0088x; 1.0088x over previous
import numpy as np
from contextlib import ExitStack

import concourse.bass as bass
import concourse.mybir as mybir
from concourse.bass_utils import run_bass_kernel_spmd

F32 = mybir.dt.float32
BF16 = mybir.dt.bfloat16
I32 = mybir.dt.int32
AX = mybir.AxisListType.X
OP = mybir.AluOpType
AF = mybir.ActivationFunctionType

L = 4096
D = 1024
DI = 2048
FI = 4096
T = 512
NCH = L // T
TPC = T // 128


class Sem:
    def __init__(self, es, nc, name):
        self.h = es.enter_context(nc.semaphore(name))
        self.n = 0

    def inc(self, instr, k=1):
        instr.then_inc(self.h, k)
        self.n += k
        return instr


class WStream:

    def __init__(self, es, nc, name, buf, slots, cols):
        self.buf, self.slots, self.cols = buf, slots, cols
        self.cons = []
        self.sems = [Sem(es, nc, f"{name}_s{i}") for i in range(slots)]
        self.first_waits = None

    def slot(self, i):
        c = (i % self.slots) * self.cols
        return self.buf[:, c:c + self.cols]


def build_program(stage="full", dbg=None):
    nc = bass.Bass("TRN2", target_bir_lowering=False,
                   detect_race_conditions=False)

    def din(name, shape, dtype=F32):
        return nc.dram_tensor(name, list(shape), dtype, kind="ExternalInput").ap()

    d_x = din("x_in", (L, D))
    d_win = din("w_in", (D, FI))
    d_wx = din("w_x", (DI, 32))
    d_wdtb = din("w_dtb", (48, DI))
    d_bdt = din("b_dt_col", (128, 16))
    d_dpar = din("d_col", (128, 16))
    d_conv = din("conv_col", (128, 64))
    d_wout = din("w_out", (DI, D))
    d_wg = din("w_gate", (FI, D))
    d_wu = din("w_up", (FI, D))
    d_wd = din("w_down", (D, FI))
    d_idf = din("idf", (128, 128))
    d_idb = din("idb", (128, 128), BF16)
    d_onerow = din("onerow", (1, 128))
    d_onecol = din("onecol", (128, 1))
    d_consts = din("consts", (128, 4))

    d_wgT = nc.dram_tensor("wgT", [D, FI], BF16).ap()
    d_wuT = nc.dram_tensor("wuT", [D, FI], BF16).ap()
    d_wdT = nc.dram_tensor("wdT", [FI, D], BF16).ap()
    d_winH = nc.dram_tensor("winH", [D, FI], BF16).ap()
    d_winL = nc.dram_tensor("winL", [D, FI], BF16).ap()
    d_y = nc.dram_tensor("y_out", [L, D], F32, kind="ExternalOutput").ap()
    d_dbg = nc.dram_tensor("dbg_out", [128, 16384], F32, kind="ExternalOutput").ap()

    es = ExitStack()
    s_gq = Sem(es, nc, "s_gq")
    s_sq = Sem(es, nc, "s_sq")
    s_pec = Sem(es, nc, "s_pec")
    s_mm = Sem(es, nc, "s_mm")
    s_ev = Sem(es, nc, "s_ev")
    s_v = Sem(es, nc, "s_v")
    s_sc = Sem(es, nc, "s_sc")
    s_st = Sem(es, nc, "s_st")

    _uid = [0]
    _dbg_ts = []

    def uname(n):
        _uid[0] += 1
        return f"{n}_{_uid[0]}"

    def sb(name, shape, dtype=F32):
        return es.enter_context(nc.sbuf_tensor(name, list(shape), dtype))

    X = sb("X", (128, TPC * D))
    STAT = sb("STAT", (128, 96))
    SCW = sb("SCW", (128, 8))
    H = sb("H", (128, 16))
    HIST = sb("HIST", (128, 48))
    BDT = sb("BDT", (128, 16))
    DPAR = sb("DPAR", (128, 16))
    CONVW = sb("CONVW", (128, 64))
    IDF = sb("IDF", (128, 128))
    IDB = sb("IDB", (128, 128), BF16)
    ONEROW = sb("ONEROW", (1, 128))
    ONECOL = sb("ONECOL", (128, 1))
    WXS = sb("WXS", (128, 512))
    WDTB = sb("WDTB", (48, DI))
    QADUM = sb("QADUM", (128, 1024))
    CONSTS = sb("CONSTS", (128, 4))

    block = es.enter_context(nc.Block())
    EN = dict(sync=block.sync, gpsimd=block.gpsimd, tensor=block.tensor,
              vector=block.vector, scalar=block.scalar)

    def emit(eng, fn):
        EN[eng](fn)

    C_MEAN, C_RSTD, C_AM, C_Q1, C_GSC, C_C2, C_S2, C_D1 = 0, 4, 8, 12, 16, 20, 24, 28
    C_C, C_VS, C_T1, C_T2, C_BND, C_AM2, C_S1 = 32, 36, 40, 44, 48, 52, 56

    def scol(base, tt):
        return STAT[:, base + tt:base + tt + 1]

    nqg = {"mm": 0}
    xlng = {"mm": 0}

    def vv(v, instr):
        s_v.inc(instr)
        v.wait_ge(s_v.h, s_v.n)

    def ss(s, instr):
        s_sc.inc(instr)
        s.wait_ge(s_sc.h, s_sc.n)

    def pewarm(t, PW, waits):
        for h, nv in waits:
            t.wait_ge(h, nv)
            t.matmul(PW[0:1, 0:1], ONECOL[:], ONECOL[:],
                     start=True, stop=True)

    def stream_loads(stream, tiles, n_groups, queue="sync", mm_final=None):
        pec_start = s_pec.n
        if mm_final is None:
            mm_final = s_mm.n + n_groups
        n = len(tiles)
        base_idx = len(stream.cons)
        for j in range(n - 1):
            stream.cons.append((s_pec.h, pec_start + j + 1))
        stream.cons.append((s_mm.h, mm_final))

        tile_waits = []
        cnt = {s: stream.sems[s].n for s in range(stream.slots)}
        for j in range(n):
            s = (base_idx + j) % stream.slots
            cnt[s] += 16
            tile_waits.append((stream.sems[s].h, cnt[s]))

        def f(q):
            if stream.first_waits is not None:
                for h, nv in stream.first_waits:
                    q.wait_ge(h, nv)
                stream.first_waits = None
            for j, t_ap in enumerate(tiles):
                gi = base_idx + j
                if gi >= stream.slots:
                    sem, val = stream.cons[gi - stream.slots]
                    q.wait_ge(sem, val)
                stream.sems[gi % stream.slots].inc(
                    q.dma_start(out=stream.slot(gi), in_=t_ap), 16)
        emit(queue, f)
        return tile_waits, base_idx

    def prologue_smalls():
        def f(g):
            for sbuf, dram in [(IDF, d_idf), (IDB, d_idb), (ONEROW, d_onerow),
                               (ONECOL, d_onecol), (BDT, d_bdt), (DPAR, d_dpar),
                               (CONVW, d_conv), (WDTB, d_wdtb),
                               (CONSTS, d_consts)]:
                s_gq.inc(g.dma_start(out=sbuf[:], in_=dram[:]), 16)
            for ci in range(16):
                s_gq.inc(g.dma_start(out=WXS[:, ci * 32:(ci + 1) * 32],
                                     in_=d_wx[ci * 128:(ci + 1) * 128, :]), 16)
        emit("gpsimd", f)
        gq_smalls = s_gq.n

        def w(e):
            e.wait_ge(s_gq.h, gq_smalls)
        for eng in ("tensor", "vector", "scalar"):
            emit(eng, w)

        def fv(v):
            v.memset(H[:], 0.0)
            v.memset(HIST[:], 0.0)
            s_v.inc(v.memset(STAT[:], 0.0))
        emit("vector", fv)

    def prologue_ffn_weights():
        specs = [(d_wg, d_wgT, 32, D, 8, 0),
                 (d_wu, d_wuT, 32, D, 8, 1),
                 (d_wd, d_wdT, 8, FI, 32, 2)]
        pa = [Sem(es, nc, "pa0"), Sem(es, nc, "pa1")]
        ts = [Sem(es, nc, "ts0"), Sem(es, nc, "ts1")]
        _dbg_ts.extend(ts)
        with (nc.sbuf_tensor(uname("PTILE"), [128, 2 * FI], F32) as PTILE,
              nc.sbuf_tensor(uname("TERN"), [128, 32 * D], BF16) as TERN,
              nc.sbuf_tensor(uname("TROW"), [128, 2 * FI], BF16) as TROW,
              nc.sbuf_tensor(uname("QP1"), [128, FI], F32) as QP1,
              nc.sbuf_tensor(uname("QPI"), [128, FI], I32) as QPI,
              nc.sbuf_tensor(uname("QP2"), [128, FI], F32) as QP2,
              nc.sbuf_tensor(uname("ACC"), [128, 4], F32) as ACC,
              nc.psum_tensor(uname("PSA"), [128, 512], F32) as PSA,
              nc.psum_tensor(uname("PTB"), [128, 1024], BF16) as PTB):
            for d_in, d_out, ntile, tcols, orows, widx in specs:
                def pslot(i):
                    return PTILE[:, (i % 2) * tcols:(i % 2) * tcols + tcols]

                v_emit = s_v.n
                paw = [(pa[i % 2].h, pa[i % 2].n + 16 * (i // 2 + 1))
                       for i in range(ntile)]

                def fload(g, d_in=d_in, ntile=ntile, tcols=tcols,
                          v_emit=v_emit):
                    for i in range(ntile):
                        if i < 2:
                            g.wait_ge(s_v.h, v_emit)
                        else:
                            g.wait_ge(s_v.h, v_emit + 2 * (i - 2) + 1)
                        pa[i % 2].inc(g.dma_start(
                            out=PTILE[:, (i % 2) * tcols:(i % 2) * tcols + tcols],
                            in_=d_in[i * 128:(i + 1) * 128, :]), 16)
                emit("sync", fload)

                def fred(v, ntile=ntile, tcols=tcols, paw=paw):
                    for i in range(ntile):
                        v.wait_ge(*paw[i])
                        vv(v, v.reduce_sum(
                            ACC[:, 1:2],
                            PTILE[:, (i % 2) * tcols:(i % 2) * tcols + tcols],
                            axis=AX, apply_absolute_value=True))
                        if i == 0:
                            vv(v, v.tensor_copy(ACC[:, 0:1], ACC[:, 1:2]))
                        else:
                            vv(v, v.tensor_tensor(ACC[:, 0:1], ACC[:, 0:1],
                                                  ACC[:, 1:2], OP.add))
                emit("vector", fred)

                def fmm(t):
                    t.wait_ge(s_v.h, s_v.n)
                    s_mm.inc(t.matmul(PSA[0:1, 0:1], ACC[:, 0:1], ONECOL[:],
                                      start=True, stop=True))
                emit("tensor", fmm)

                nelem = float(ntile * 128 * tcols)
                mm_now = s_mm.n

                def ftiny(v, nelem=nelem, mm_now=mm_now):
                    v.wait_ge(s_mm.h, mm_now)
                    vv(v, v.tensor_scalar(ACC[0:1, 2:3], PSA[0:1, 0:1],
                                          1.0 / nelem, 1e-5, OP.mult, OP.max))
                    vv(v, v.reciprocal(ACC[0:1, 3:4], ACC[0:1, 2:3]))
                emit("vector", ftiny)

                def fbc(t):
                    t.wait_ge(s_v.h, s_v.n)
                    s_mm.inc(t.matmul(PSA[:, 1:2], ONEROW[:], ACC[0:1, 3:4],
                                      start=True, stop=True))
                emit("tensor", fbc)

                mm_now2 = s_mm.n

                def ftiny2(v, widx=widx, mm_now2=mm_now2):
                    v.wait_ge(s_mm.h, mm_now2)
                    vv(v, v.tensor_copy(SCW[:, 3 + widx:4 + widx], PSA[:, 1:2]))
                    vv(v, v.reciprocal(SCW[:, widx:widx + 1], PSA[:, 1:2]))
                emit("vector", ftiny2)

                v_emit2 = s_v.n
                mm_guard = s_mm.n
                paw2 = [(pa[i % 2].h, pa[i % 2].n + 16 * (i // 2 + 1))
                        for i in range(ntile)]

                def fload2(g, d_in=d_in, ntile=ntile, tcols=tcols,
                           v_emit2=v_emit2):
                    for i in range(ntile):
                        if i < 2:
                            g.wait_ge(s_v.h, v_emit2)
                        else:
                            g.wait_ge(s_v.h, v_emit2 + 2 * (i - 2) + 2)
                        pa[i % 2].inc(g.dma_start(
                            out=PTILE[:, (i % 2) * tcols:(i % 2) * tcols + tcols],
                            in_=d_in[i * 128:(i + 1) * 128, :]), 16)
                emit("sync", fload2)

                def fq(v, ntile=ntile, tcols=tcols, widx=widx,
                       paw2=paw2, mm_guard=mm_guard):
                    v.wait_ge(s_mm.h, mm_guard)
                    v.wait_ge(s_v.h, s_v.n)
                    for i in range(ntile):
                        v.wait_ge(*paw2[i])
                        src = PTILE[:, (i % 2) * tcols:(i % 2) * tcols + tcols]
                        v.tensor_scalar(QP1[:, :tcols], src,
                                        SCW[:, 3 + widx:4 + widx], 0.5,
                                        OP.mult, OP.add)
                        v.tensor_copy(QPI[:, :tcols], QP1[:, :tcols])
                        v.tensor_tensor(QP1[:, :tcols], QPI[:, :tcols],
                                        QP1[:, :tcols], OP.is_gt)
                        s_v.inc(v.tensor_tensor(QP2[:, :tcols], QPI[:, :tcols],
                                                QP1[:, :tcols], OP.subtract))
                        s_v.inc(v.tensor_scalar(
                            TERN[:, i * tcols:i * tcols + tcols],
                            QP2[:, :tcols], -1.0, 1.0, OP.max, OP.min))
                emit("vector", fq)

                if dbg == "tern" and widx == 0:
                    vfin0 = s_v.n

                    def dtap(v):
                        v.wait_ge(s_v.h, vfin0)
                        v.tensor_copy(QP1[:, 0:4096], TERN[:, 0:4096])
                        s_v.inc(v.tensor_copy(QP2[:, 0:4096], TERN[:, 4096:8192]))
                    emit("vector", dtap)
                    vfin = s_v.n

                    def dst(g):
                        g.wait_ge(s_v.h, vfin)
                        s_st.inc(g.dma_start(out=d_dbg[:, 0:4096], in_=QP1[:, 0:4096]), 16)
                        s_st.inc(g.dma_start(out=d_dbg[:, 4096:8192], in_=QP2[:, 0:4096]), 16)
                    emit("gpsimd", dst)

                st_marks = {}
                for j in range(orows):
                    vb = s_v.n
                    mmb = s_mm.n

                    scb_tr = s_sc.n

                    def ftr(t, j=j, ntile=ntile, tcols=tcols, vb=vb,
                            scb_tr=scb_tr):
                        t.wait_ge(s_v.h, vb)
                        t.wait_ge(s_sc.h, scb_tr)
                        for i in range(ntile):
                            if i >= 8:
                                t.wait_ge(s_sc.h, scb_tr + (i // 8))
                            s_mm.inc(t.transpose(
                                PTB[:, (i % 8) * 128:(i % 8) * 128 + 128],
                                TERN[:, i * tcols + j * 128:
                                     i * tcols + j * 128 + 128],
                                IDB[:]))
                    emit("tensor", ftr)

                    trow_half = (j % 2) * FI

                    tsnow = ts[j % 2].n

                    def fev(s, j=j, ntile=ntile, mmb=mmb, trow_half=trow_half,
                            st_marks=st_marks, tsnow=tsnow):
                        if j >= 2 and (j - 2) in st_marks:
                            s.wait_ge(*st_marks[j - 2])
                        elif j < 2:
                            s.wait_ge(ts[j % 2].h, tsnow)
                        ng = (ntile + 7) // 8
                        for gi in range(ng):
                            cnt = min(8, ntile - gi * 8)
                            s.wait_ge(s_mm.h, mmb + gi * 8 + cnt)
                            s_sc.inc(s.copy(
                                TROW[:, trow_half + gi * 1024:
                                     trow_half + gi * 1024 + cnt * 128],
                                PTB[:, :cnt * 128]))
                    emit("scalar", fev)

                    scnow = s_sc.n

                    def fst(g, j=j, d_out=d_out, ntile=ntile, scnow=scnow,
                            trow_half=trow_half):
                        g.wait_ge(s_sc.h, scnow)
                        ts[j % 2].inc(g.dma_start(
                            out=d_out[j * 128:(j + 1) * 128, :],
                            in_=TROW[:, trow_half:trow_half + ntile * 128]), 16)
                    emit("gpsimd", fst)
                    st_marks[j] = (ts[j % 2].h, ts[j % 2].n)

            v_emit3 = s_v.n
            paw3 = [(pa[i % 2].h, pa[i % 2].n + 16 * (i // 2 + 1))
                    for i in range(8)]

            def wload3(g, v_emit3=v_emit3):
                for i in range(8):
                    if i < 2:
                        g.wait_ge(s_v.h, v_emit3)
                    else:
                        g.wait_ge(s_v.h, v_emit3 + (i - 2) + 1)
                    pa[i % 2].inc(g.dma_start(
                        out=PTILE[:, (i % 2) * FI:(i % 2) * FI + FI],
                        in_=d_win[i * 128:(i + 1) * 128, :]), 16)
            emit("sync", wload3)

            st3 = {}
            for i in range(8):
                ts0n, ts1n = ts[0].n, ts[1].n
                srcw = PTILE[:, (i % 2) * FI:(i % 2) * FI + FI]

                def wsplitS(s, i=i, ts0n=ts0n, srcw=srcw, v_emit3=v_emit3):
                    s.wait_ge(*paw3[i])
                    if i >= 1:
                        s.wait_ge(*st3[i - 1][0])
                        s.wait_ge(s_v.h, v_emit3 + i)
                    else:
                        s.wait_ge(ts[0].h, ts0n)
                        s.wait_ge(s_v.h, v_emit3)
                    s_sc.inc(s.copy(TROW[:, 0:FI], srcw))
                    s_sc.inc(s.copy(QP1[:, :FI], TROW[:, 0:FI]))
                emit("scalar", wsplitS)

                scn3 = s_sc.n

                def wsplitV(v, i=i, ts1n=ts1n, srcw=srcw, scn3=scn3):
                    v.wait_ge(s_sc.h, scn3)
                    if i >= 1:
                        v.wait_ge(*st3[i - 1][1])
                    else:
                        v.wait_ge(ts[1].h, ts1n)
                    s_v.inc(v.tensor_tensor(TROW[:, FI:2 * FI], srcw,
                                            QP1[:, :FI], OP.subtract))
                emit("vector", wsplitV)

                vnow3 = s_v.n

                def wst3(g, i=i, vnow3=vnow3):
                    g.wait_ge(s_v.h, vnow3)
                    ts[0].inc(g.dma_start(out=d_winH[i * 128:(i + 1) * 128, :],
                                          in_=TROW[:, 0:FI]), 16)
                    ts[1].inc(g.dma_start(out=d_winL[i * 128:(i + 1) * 128, :],
                                          in_=TROW[:, FI:2 * FI]), 16)
                emit("gpsimd", wst3)
                st3[i] = [(ts[0].h, ts[0].n), (ts[1].h, ts[1].n)]

    def load_x(k, first):
        stn = s_st.n

        def f(g, k=k, first=first, stn=stn):
            if not first:
                g.wait_ge(s_st.h, stn)
            for tt in range(TPC):
                s_gq.inc(g.dma_start(
                    out=X[:, tt * D:(tt + 1) * D],
                    in_=d_x[k * T + tt * 128:k * T + tt * 128 + 128, :]), 16)
        emit("gpsimd", f)
        return s_gq.n

    def store_x(k):
        vn = s_v.n

        def f(g, k=k, vn=vn):
            g.wait_ge(s_v.h, vn)
            for tt in range(TPC):
                s_st.inc(g.dma_start(
                    out=d_y[k * T + tt * 128:k * T + tt * 128 + 128, :],
                    in_=X[:, tt * D:(tt + 1) * D]), 16)
        emit("gpsimd", f)

    def ffn(k, xload_gq, wstreams):
        wg_s, wu_s, wd_s = wstreams
        with (nc.sbuf_tensor(uname("WGb"), [128, 4 * 1024], BF16) as WGb,
              nc.sbuf_tensor(uname("WUb"), [128, 4 * 1024], BF16) as WUb,
              nc.sbuf_tensor(uname("WDb"), [128, 4 * 1024], BF16) as WDb,
              nc.sbuf_tensor(uname("PROD"), [128, TPC * FI], F32) as PROD,
              nc.sbuf_tensor(uname("QA"), [128, 1024], F32) as QA,
              nc.sbuf_tensor(uname("C32"), [128, 1024], I32) as C32,
              nc.sbuf_tensor(uname("QC"), [128, 1024], F32) as QC,
              nc.sbuf_tensor(uname("XCB"), [128, 1024], F32) as XCB,
              nc.sbuf_tensor(uname("SG"), [128, TPC * 1024], F32) as SG,
              nc.sbuf_tensor(uname("N1T"), [128, 4096], BF16) as N1T,
              nc.sbuf_tensor(uname("N2T"), [128, 16384], BF16) as N2T,
              nc.sbuf_tensor(uname("NQ"), [128, 4096], BF16) as NQ):
            wg_s.buf, wu_s.buf, wd_s.buf = WGb, WUb, WDb

            with (nc.psum_tensor(uname("PTB1"), [128, 1024], BF16) as PTB1,
                  nc.psum_tensor(uname("PW1"), [128, 8], F32) as PW1):
                def f1h(v):
                    v.wait_ge(s_gq.h, xload_gq)
                    vv(v, v.reduce_sum(STAT[:, C_T1:C_T1 + 4],
                                       X.rearrange("p (t d) -> p t d", d=D),
                                       axis=AX))
                    vv(v, v.tensor_scalar(STAT[:, C_MEAN:C_MEAN + 4],
                                          STAT[:, C_T1:C_T1 + 4],
                                          1.0 / D, None, OP.mult))
                emit("vector", f1h)

                for tt in range(TPC):
                    xs = X[:, tt * D:(tt + 1) * D]

                    def f1v(v, tt=tt, xs=xs):
                        v.wait_ge(s_sc.h, s_sc.n)
                        s_v.inc(v.tensor_scalar(XCB[:], xs, scol(C_MEAN, tt),
                                                None, OP.subtract))
                    emit("vector", f1v)
                    a1 = s_v.n

                    def f1s(s, tt=tt):
                        s.wait_ge(s_v.h, s_v.n)
                        ss(s, s.activation(QADUM[:], XCB[:], AF.Square,
                                           accum_out=scol(C_VS, tt)))
                        ss(s, s.activation(scol(C_T1, tt), scol(C_VS, tt),
                                           AF.Ln, bias=CONSTS[:, 0:1],
                                           scale=1.0 / D))
                        s_sc.inc(s.activation(scol(C_RSTD, tt), scol(C_T1, tt),
                                              AF.Exp, bias=0.0, scale=-0.5))
                    emit("scalar", f1s)
                    c1 = s_sc.n
                    b1 = s_v.n

                    def f1v2(v, tt=tt):
                        vv(v, v.reduce_max(scol(C_AM, tt), XCB[:], axis=AX,
                                           apply_absolute_value=True))
                        v.wait_ge(s_sc.h, s_sc.n)
                        vv(v, v.tensor_scalar(scol(C_AM2, tt), scol(C_AM, tt),
                                              scol(C_RSTD, tt), 1e-5,
                                              OP.mult, OP.max))
                        vv(v, v.reciprocal(scol(C_T1, tt), scol(C_AM2, tt)))
                        vv(v, v.tensor_scalar(scol(C_S1, tt), scol(C_T1, tt),
                                              127.0, None, OP.mult))
                        vv(v, v.tensor_scalar(scol(C_Q1, tt), scol(C_S1, tt),
                                              scol(C_RSTD, tt), None, OP.mult))
                        vv(v, v.reciprocal(scol(C_D1, tt), scol(C_S1, tt)))
                        vv(v, v.tensor_scalar(scol(C_GSC, tt), scol(C_D1, tt),
                                              SCW[:, 0:1], None, OP.mult))
                        vv(v, v.tensor_scalar(scol(C_C, tt), scol(C_D1, tt),
                                              SCW[:, 1:2], None, OP.mult))
                        vv(v, v.reciprocal(scol(C_T2, tt), scol(C_C, tt)))
                        vv(v, v.tensor_scalar(scol(C_BND, tt), scol(C_T2, tt),
                                              1e-5, None, OP.mult))
                        v.wait_ge(s_mm.h, nqg["mm"])
                        for hf in range(2):
                            c0 = hf * 512
                            v.tensor_scalar(QA[:, c0:c0 + 512],
                                            XCB[:, c0:c0 + 512],
                                            scol(C_Q1, tt), 0.5,
                                            OP.mult, OP.add)
                            v.tensor_copy(C32[:, c0:c0 + 512], QA[:, c0:c0 + 512])
                            v.tensor_tensor(QA[:, c0:c0 + 512], C32[:, c0:c0 + 512],
                                            QA[:, c0:c0 + 512], OP.is_gt)
                            s_v.inc(v.tensor_tensor(
                                NQ[:, c0:c0 + 512], C32[:, c0:c0 + 512],
                                QA[:, c0:c0 + 512], OP.subtract))
                    emit("vector", f1v2)

                    mmb = s_mm.n
                    evp1 = s_ev.n
                    vfin1 = s_v.n

                    def f1t(t, tt=tt, evp1=evp1, vfin1=vfin1,
                            a1=a1, b1=b1, c1=c1):
                        pewarm(t, PW1, [(s_v.h, max(0, a1 - 4)), (s_v.h, a1),
                                        (s_sc.h, c1), (s_v.h, b1 + 5),
                                        (s_v.h, b1 + 10)])
                        t.wait_ge(s_ev.h, evp1)
                        for hf in range(2):
                            t.wait_ge(s_v.h, vfin1 - 1 + hf)
                            for f in range(4):
                                fb = hf * 4 + f
                                s_mm.inc(t.transpose(
                                    PTB1[:, fb * 128:(fb + 1) * 128],
                                    NQ[:, fb * 128:(fb + 1) * 128], IDB[:]))
                    emit("tensor", f1t)
                    nqg["mm"] = s_mm.n

                    def f1e(s, tt=tt, mmb=mmb):
                        s.wait_ge(s_mm.h, mmb + 8)
                        s_ev.inc(s.copy(
                            N1T[:, tt * 1024:(tt + 1) * 1024], PTB1[:]))
                    emit("scalar", f1e)

            with ExitStack() as pse:
                PS = [pse.enter_context(
                    nc.psum_tensor(uname(f"PSF{i}"), [128, 512], F32)) for i in range(8)]

                n1t_ev = s_ev.n
                prev = {"evg": None, "vpr": None}
                for q in range(4):
                    for wsel in ("g", "u"):
                        stream = wg_s if wsel == "g" else wu_s
                        dscr = d_wgT if wsel == "g" else d_wuT
                        tiles = [dscr[f * 128:(f + 1) * 128, q * 1024:(q + 1) * 1024]
                                 for f in range(8)]
                        tile_waits, bidx = stream_loads(stream, tiles, 8)

                        vb = s_v.n
                        mm0 = s_mm.n
                        evg, vpr = prev["evg"], prev["vpr"]

                        def fmm(t, q=q, wsel=wsel, stream=stream, bidx=bidx,
                                tile_waits=tile_waits, evg=evg, vpr=vpr, vb=vb,
                                n1t_ev=n1t_ev):
                            for f in range(8):
                                t.wait_ge(*tile_waits[f])
                                if f == 0 and wsel == "g" and q == 0:
                                    if vpr is None:
                                        t.wait_ge(s_v.h, vb)
                                    t.wait_ge(s_ev.h, n1t_ev)
                                w = stream.slot(bidx + f)
                                ins = None
                                for tt in range(TPC):
                                    lhsT = N1T[:, tt * 1024 + f * 128:
                                               tt * 1024 + f * 128 + 128]
                                    for ncn in range(2):
                                        g = tt * 2 + ncn
                                        if f == 0:
                                            if wsel == "u":
                                                t.wait_ge(s_ev.h, evg + g + 1)
                                            elif vpr is not None:
                                                t.wait_ge(s_v.h, vpr + g + 1)
                                        ins = t.matmul(
                                            PS[g][:], lhsT,
                                            w[:, ncn * 512:ncn * 512 + 512],
                                            start=(f == 0), stop=(f == 7))
                                        if f == 7:
                                            s_mm.inc(ins)
                                if f < 7:
                                    s_pec.inc(ins)
                        emit("tensor", fmm)

                        if wsel == "g":
                            prev["evg"] = s_ev.n
                            def fev(s, mm0=mm0):
                                for g in range(8):
                                    s.wait_ge(s_mm.h, mm0 + g + 1)
                                    tt, ncn = g // 2, g % 2
                                    s_ev.inc(s.activation(
                                        SG[:, tt * 1024 + ncn * 512:
                                           tt * 1024 + ncn * 512 + 512],
                                        PS[g][:], AF.Silu, bias=0.0,
                                        scale=scol(C_GSC, tt)))
                            emit("scalar", fev)
                        else:
                            prev["vpr"] = s_v.n

                            def fpr(v, q=q, mm0=mm0):
                                for g in range(8):
                                    v.wait_ge(s_mm.h, mm0 + g + 1)
                                    tt, ncn = g // 2, g % 2
                                    s_v.inc(v.tensor_tensor(
                                        PROD[:, tt * FI + q * 1024 + ncn * 512:
                                             tt * FI + q * 1024 + ncn * 512 + 512],
                                        SG[:, tt * 1024 + ncn * 512:
                                           tt * 1024 + ncn * 512 + 512],
                                        PS[g][:], OP.mult))
                                for tt in range(TPC):
                                    vv(v, v.reduce_max(
                                        STAT[:, 64 + tt * 4 + q:65 + tt * 4 + q],
                                        PROD[:, tt * FI + q * 1024:
                                             tt * FI + q * 1024 + 1024],
                                        axis=AX, apply_absolute_value=True))
                            emit("vector", fpr)

            if dbg == "prod" and k == 0:
                def dvp(v):
                    s_v.inc(v.tensor_copy(QADUM[:], PROD[:, 0:1024]))
                emit("vector", dvp)
                vfin = s_v.n

                def dstp(g):
                    g.wait_ge(s_v.h, vfin)
                    s_st.inc(g.dma_start(out=d_dbg[:, 0:1024], in_=QADUM[:]), 16)
                    g.wait_ge(s_st.h, s_st.n)
                emit("gpsimd", dstp)

            with nc.psum_tensor(uname("PTB2"), [128, 1024], BF16) as PTB2:
                def f3c(v):
                    w4 = lambda b: STAT[:, b:b + 4]
                    vv(v, v.reduce_max(w4(C_T1),
                                       STAT[:, 64:80].rearrange(
                                           "p (t q) -> p t q", q=4),
                                       axis=AX, apply_absolute_value=False))
                    vv(v, v.tensor_tensor(w4(C_AM2), w4(C_T1), w4(C_BND),
                                          OP.max))
                    vv(v, v.reciprocal(w4(C_T2), w4(C_AM2)))
                    vv(v, v.tensor_scalar(w4(C_S2), w4(C_T2),
                                          127.0, None, OP.mult))
                    vv(v, v.reciprocal(w4(C_T1), w4(C_S2)))
                    vv(v, v.tensor_tensor(w4(C_T2), w4(C_T1), w4(C_C),
                                          OP.mult))
                    vv(v, v.tensor_scalar(w4(C_C2), w4(C_T2),
                                          SCW[:, 2:3], 0.5, OP.mult, OP.mult))
                emit("vector", f3c)

                for tt in range(TPC):
                    pre3 = s_v.n

                    def f3v(v, tt=tt):
                        pr = PROD[:, tt * FI:(tt + 1) * FI]
                        v.wait_ge(s_mm.h, nqg["mm"])
                        for sl in range(4):
                            p = pr[:, sl * 1024:(sl + 1) * 1024]
                            v.tensor_scalar(QA[:], p, scol(C_S2, tt), 0.5,
                                            OP.mult, OP.add)
                            v.tensor_copy(C32[:], QA[:])
                            v.tensor_tensor(QA[:], C32[:], QA[:], OP.is_gt)
                            s_v.inc(v.tensor_tensor(
                                NQ[:, sl * 1024:(sl + 1) * 1024],
                                C32[:], QA[:], OP.subtract))
                    emit("vector", f3v)

                    mmb = s_mm.n
                    evb2 = s_ev.n

                    def f3t(t, tt=tt, pre3=pre3, evb2=evb2):
                        t.wait_ge(s_ev.h, evb2)
                        for blk in range(4):
                            t.wait_ge(s_v.h, pre3 + 1 + blk)
                            if blk > 0:
                                t.wait_ge(s_ev.h, evb2 + blk)
                            for f in range(8):
                                s_mm.inc(t.transpose(
                                    PTB2[:, f * 128:(f + 1) * 128],
                                    NQ[:, blk * 1024 + f * 128:
                                       blk * 1024 + f * 128 + 128], IDB[:]))
                    emit("tensor", f3t)
                    nqg["mm"] = s_mm.n

                    def f3e(s, tt=tt, mmb=mmb):
                        for blk in range(4):
                            s.wait_ge(s_mm.h, mmb + blk * 8 + 8)
                            s_ev.inc(s.copy(
                                N2T[:, tt * 4096 + blk * 1024:
                                    tt * 4096 + blk * 1024 + 1024], PTB2[:]))
                    emit("scalar", f3e)

            n2t_ready = s_v.n
            n2t_ev = s_ev.n
            with ExitStack() as pse:
                PS = [pse.enter_context(
                    nc.psum_tensor(uname(f"PSD{i}"), [128, 512], F32)) for i in range(8)]

                tiles = [d_wdT[f2 * 128:(f2 + 1) * 128, :] for f2 in range(32)]
                tile_waits, bidx = stream_loads(wd_s, tiles, 8)

                def fmm(t, tile_waits=tile_waits, bidx=bidx,
                        n2t_ready=n2t_ready, n2t_ev=n2t_ev):
                    t.wait_ge(s_v.h, n2t_ready)
                    t.wait_ge(s_ev.h, n2t_ev)
                    for f2 in range(32):
                        t.wait_ge(*tile_waits[f2])
                        w = wd_s.slot(bidx + f2)
                        ins = None
                        for tt in range(TPC):
                            lhsT = N2T[:, tt * 4096 + f2 * 128:
                                       tt * 4096 + f2 * 128 + 128]
                            for ncn in range(2):
                                ins = t.matmul(
                                    PS[tt * 2 + ncn][:], lhsT,
                                    w[:, ncn * 512:ncn * 512 + 512],
                                    start=(f2 == 0), stop=(f2 == 31))
                                if f2 == 31:
                                    s_mm.inc(ins)
                        if f2 < 31:
                            s_pec.inc(ins)
                emit("tensor", fmm)

                mm0 = s_mm.n - 8

                def fres(v, mm0=mm0):
                    for g in range(8):
                        v.wait_ge(s_mm.h, mm0 + g + 1)
                        tt, ncn = g // 2, g % 2
                        xs = X[:, tt * D + ncn * 512:tt * D + ncn * 512 + 512]
                        s_v.inc(v.scalar_tensor_tensor(
                            xs, PS[g][:], scol(C_C2, tt), xs, OP.mult, OP.add))
                emit("vector", fres)

    def mamba(k, wstreams, upto="m4"):
        wi_s, wo_s = wstreams
        with (nc.sbuf_tensor(uname("WIb"), [128, 4 * 1024], BF16) as WIb,
              nc.sbuf_tensor(uname("WOb"), [128, 2 * 1024], F32) as WOb,
              nc.sbuf_tensor(uname("XCT"), [128, 16 * 515], F32) as XCT,
              nc.sbuf_tensor(uname("SZ"), [128, 16 * 512], F32) as SZ,
              nc.sbuf_tensor(uname("U"), [128, 16 * 512], F32) as U,
              nc.sbuf_tensor(uname("YS"), [128, 16 * 512], F32) as YS,
              nc.sbuf_tensor(uname("XLTH"), [128, 8 * 512], BF16) as XLTH,
              nc.sbuf_tensor(uname("XLTL"), [128, 8 * 512], BF16) as XLTL,
              nc.sbuf_tensor(uname("UBs"), [128, 2 * 512], F32) as UBs,
              nc.sbuf_tensor(uname("XPS"), [48, 512], F32) as XPS,
              nc.sbuf_tensor(uname("TM"), [128, 2 * 512], F32) as TM,
              nc.sbuf_tensor(uname("XLN"), [128, 1024], F32) as XLN):
            wi_s.buf, wo_s.buf = WIb, WOb
            EXPS = QADUM

            def xct(ci):
                return XCT[:, ci * 515:ci * 515 + 515]

            with (nc.psum_tensor(uname("PTF"), [128, 1024], F32) as PTF,
                  nc.psum_tensor(uname("PW4"), [128, 8], F32) as PW4):
                def m0h(v):
                    vv(v, v.reduce_sum(STAT[:, C_T1:C_T1 + 4],
                                       X.rearrange("p (t d) -> p t d", d=D),
                                       axis=AX))
                    vv(v, v.tensor_scalar(STAT[:, C_MEAN:C_MEAN + 4],
                                          STAT[:, C_T1:C_T1 + 4],
                                          1.0 / D, None, OP.mult))
                emit("vector", m0h)

                for tt in range(TPC):
                    xs = X[:, tt * D:(tt + 1) * D]

                    def m0v(v, tt=tt, xs=xs):
                        v.wait_ge(s_sc.h, s_sc.n)
                        v.wait_ge(s_mm.h, xlng["mm"])
                        s_v.inc(v.tensor_scalar(XLN[:], xs, scol(C_MEAN, tt),
                                                None, OP.subtract))
                    emit("vector", m0v)
                    a2 = s_v.n

                    def m0s(s, tt=tt):
                        s.wait_ge(s_v.h, s_v.n)
                        ss(s, s.activation(QADUM[:], XLN[:], AF.Square,
                                           accum_out=scol(C_VS, tt)))
                        ss(s, s.activation(scol(C_T1, tt), scol(C_VS, tt),
                                           AF.Ln, bias=CONSTS[:, 0:1],
                                           scale=1.0 / D))
                        s_sc.inc(s.activation(scol(C_RSTD, tt), scol(C_T1, tt),
                                              AF.Exp, bias=0.0, scale=-0.5))
                    emit("scalar", m0s)
                    c2m = s_sc.n

                    def m0v2(v, tt=tt):
                        v.wait_ge(s_sc.h, s_sc.n)
                        s_v.inc(v.tensor_scalar(XLN[:], XLN[:],
                                                scol(C_RSTD, tt), None, OP.mult))
                    emit("vector", m0v2)

                    mmb = s_mm.n
                    evp = s_ev.n

                    def m0t(t, tt=tt, evp=evp, a2=a2, c2m=c2m, vfin=s_v.n):
                        pewarm(t, PW4, [(s_v.h, max(0, a2 - 6)),
                                        (s_v.h, max(0, a2 - 3)), (s_v.h, a2),
                                        (s_sc.h, c2m)])
                        t.wait_ge(s_v.h, vfin)
                        t.wait_ge(s_ev.h, evp)
                        for f in range(8):
                            s_mm.inc(t.transpose(PTF[:, f * 128:(f + 1) * 128],
                                                 XLN[:, f * 128:(f + 1) * 128],
                                                 IDF[:]))
                    emit("tensor", m0t)
                    xlng["mm"] = s_mm.n

                    def m0e(s, tt=tt, mmb=mmb):
                        s.wait_ge(s_mm.h, mmb + 8)
                        dst = XLTH.rearrange("p (f c) -> p f c", c=512)[
                            :, :, tt * 128:tt * 128 + 128]
                        src = PTF.rearrange("p (f c) -> p f c", c=128)
                        s_ev.inc(s.copy(dst, src))
                    emit("scalar", m0e)

                    evh = s_ev.n

                    def m0l(v, tt=tt, evh=evh):
                        v.wait_ge(s_ev.h, evh)
                        dstH = XLTH.rearrange("p (f c) -> p f c", c=512)[
                            :, :, tt * 128:tt * 128 + 128]
                        dstL = XLTL.rearrange("p (f c) -> p f c", c=512)[
                            :, :, tt * 128:tt * 128 + 128]
                        qv = QADUM.rearrange("p (f c) -> p f c", c=128)
                        src = PTF.rearrange("p (f c) -> p f c", c=128)
                        s_v.inc(v.tensor_copy(qv, dstH))
                        s_v.inc(v.tensor_tensor(dstL, src, qv, OP.subtract))
                    emit("vector", m0l)

            if upto == "m0":
                return
            xc_ev_marks = {}
            xlt_ready = s_v.n
            xlt_ev = s_ev.n
            with ExitStack() as pse:
                PS = [pse.enter_context(
                    nc.psum_tensor(uname(f"PSM{i}"), [128, 512], F32)) for i in range(8)]

                for mg in range(4):
                    tiles = []
                    for f in range(8):
                        tiles.append(d_winH[f * 128:(f + 1) * 128,
                                            mg * 1024:(mg + 1) * 1024])
                        tiles.append(d_winL[f * 128:(f + 1) * 128,
                                            mg * 1024:(mg + 1) * 1024])
                    tile_waits, bidx = stream_loads(wi_s, tiles, 8)
                    evb = s_ev.n

                    def fmm(t, mg=mg, tile_waits=tile_waits, bidx=bidx, evb=evb,
                            xlt_ready=xlt_ready, xlt_ev=xlt_ev):
                        if mg == 0:
                            t.wait_ge(s_v.h, xlt_ready)
                            t.wait_ge(s_ev.h, xlt_ev)
                        else:
                            t.wait_ge(s_ev.h, evb)
                        for f in range(8):
                            t.wait_ge(*tile_waits[2 * f])
                            t.wait_ge(*tile_waits[2 * f + 1])
                            wh = wi_s.slot(bidx + 2 * f)
                            wl = wi_s.slot(bidx + 2 * f + 1)
                            rh = XLTH[:, f * 512:(f + 1) * 512]
                            rl = XLTL[:, f * 512:(f + 1) * 512]
                            for m8 in range(8):
                                whc = wh[:, m8 * 128:m8 * 128 + 128]
                                wlc = wl[:, m8 * 128:m8 * 128 + 128]
                                t.matmul(PS[m8][:], whc, rh,
                                         start=(f == 0), stop=False)
                                i2 = t.matmul(PS[m8][:], whc, rl,
                                              start=False, stop=False)
                                i3 = t.matmul(PS[m8][:], wlc, rh,
                                              start=False, stop=(f == 7))
                                if f == 7:
                                    s_mm.inc(i3)
                                elif m8 == 7:
                                    s_pec.inc(i3)
                                if m8 == 7:
                                    s_pec.inc(i2)
                    emit("tensor", fmm)

                    mm0 = s_mm.n - 8

                    def fev(s, mg=mg, mm0=mm0):
                        for m8 in range(8):
                            m = mg * 8 + m8
                            s.wait_ge(s_mm.h, mm0 + m8 + 1)
                            if m < 16:
                                s_ev.inc(s.copy(xct(m)[:, 3:515], PS[m8][:]))
                            else:
                                s_ev.inc(s.activation(
                                    SZ[:, (m - 16) * 512:(m - 15) * 512],
                                    PS[m8][:], AF.Silu))
                            if m < 16:
                                xc_ev_marks[m] = s_ev.n
                    emit("scalar", fev)

            if upto == "m1":
                return
            ev_all = s_ev.n
            v_conv0 = s_v.n
            sc_silu0 = s_sc.n

            def m2v(v, ev_all=ev_all, sc_silu0=sc_silu0):
                for ci in range(16):
                    v.wait_ge(s_ev.h, xc_ev_marks[ci])
                    if ci >= 2:
                        v.wait_ge(s_sc.h, sc_silu0 + (ci - 2) + 1)
                    xc = xct(ci)
                    e = EXPS[:, (ci % 2) * 512:(ci % 2) * 512 + 512]
                    vv(v, v.tensor_copy(xc[:, 0:3], HIST[:, ci * 3:ci * 3 + 3]))
                    cw = CONVW[:, ci * 4:ci * 4 + 4]
                    v.tensor_scalar(e, xc[:, 3:515], cw[:, 3:4], None, OP.mult)
                    v.scalar_tensor_tensor(e, xc[:, 2:514], cw[:, 2:3], e,
                                           OP.mult, OP.add)
                    v.scalar_tensor_tensor(e, xc[:, 1:513], cw[:, 1:2], e,
                                           OP.mult, OP.add)
                    s_v.inc(v.scalar_tensor_tensor(e, xc[:, 0:512], cw[:, 0:1],
                                                   e, OP.mult, OP.add))
                    v.tensor_copy(HIST[:, ci * 3:ci * 3 + 3], xc[:, 512:515])
            emit("vector", m2v)

            def m2s(s, v_conv0=v_conv0):
                for ci in range(16):
                    s.wait_ge(s_v.h, v_conv0 + 2 * ci + 2)
                    e = EXPS[:, (ci % 2) * 512:(ci % 2) * 512 + 512]
                    s_sc.inc(s.activation(U[:, ci * 512:(ci + 1) * 512],
                                          e, AF.Silu))
            emit("scalar", m2s)

            with nc.psum_tensor(uname("PXP"), [48, 512], F32) as PXP:
                sc_after = s_sc.n

                def m2t(t, sc_after=sc_after):
                    for ci in range(16):
                        t.wait_ge(s_sc.h, sc_after - (15 - ci))
                        s_mm.inc(t.matmul(PXP[0:16, :],
                                          WXS[:, ci * 32:ci * 32 + 16],
                                          U[:, ci * 512:(ci + 1) * 512],
                                          start=(ci == 0), stop=(ci == 15)))
                        s_mm.inc(t.matmul(PXP[32:48, :],
                                          WXS[:, ci * 32 + 16:ci * 32 + 32],
                                          U[:, ci * 512:(ci + 1) * 512],
                                          start=(ci == 0), stop=(ci == 15)))
                emit("tensor", m2t)

                mm_now = s_mm.n

                def m2e(v, mm_now=mm_now):
                    v.wait_ge(s_mm.h, mm_now)
                    vv(v, v.tensor_copy(XPS[0:16, :], PXP[0:16, :]))
                    vv(v, v.tensor_copy(XPS[32:48, :], PXP[32:48, :]))
                emit("vector", m2e)

            if upto == "m2":
                return
            pse_w = ExitStack()
            PSA4 = [pse_w.enter_context(
                nc.psum_tensor(uname(f"PSW{i}"), [128, 512], F32))
                for i in range(4)]
            with (nc.psum_tensor(uname("PDT"), [128, 512], F32) as PDT,
                  nc.psum_tensor(uname("PBP"), [128, 512], F32) as PBP):
                tiles = [d_wout[ci * 128:(ci + 1) * 128, :] for ci in range(16)]
                wo_waits, wo_bidx = stream_loads(wo_s, tiles, 4,
                                                 mm_final=s_mm.n + 36)
                y_marks = {}
                xp_ready = s_v.n
                mm0s = {}
                vpre = {}

                def emit_m3t(ci):
                    mm0s[ci] = s_mm.n
                    if ci == 0:
                        waits = [(s_v.h, xp_ready)]
                    else:
                        waits = [(s_sc.h, s_sc.n), (s_v.h, vpre[ci - 1] + 1)]

                    def m3t(t, ci=ci, waits=waits):
                        for h, nv in waits:
                            t.wait_ge(h, nv)
                        s_mm.inc(t.matmul(PDT[:],
                                          WDTB[0:16, ci * 128:ci * 128 + 128],
                                          XPS[0:16, :], start=True, stop=True))
                        s_mm.inc(t.matmul(PBP[:],
                                          WDTB[32:48, ci * 128:ci * 128 + 128],
                                          XPS[32:48, :], start=True, stop=True))
                    emit("tensor", m3t)

                emit_m3t(0)
                for ci in range(16):
                    mm0 = mm0s[ci]

                    def m3s(s, ci=ci, mm0=mm0):
                        s.wait_ge(s_mm.h, mm0 + 1)
                        s.activation(EXPS[:, 0:512], PDT[:], AF.Exp,
                                     bias=BDT[:, ci:ci + 1])
                        s_sc.inc(s.activation(xct(ci)[:, 3:515], EXPS[:, 0:512],
                                              AF.Ln, bias=1.0))
                    emit("scalar", m3s)

                    vpre[ci] = s_v.n

                    def m3v(v, ci=ci, mm0=mm0):
                        v.wait_ge(s_sc.h, s_sc.n)
                        a = xct(ci)[:, 3:515]
                        v.tensor_scalar(a, a, 1e-4, 1.0, OP.max, OP.min)
                        v.tensor_scalar(a, a, -1.0, 1.0, OP.mult, OP.add)
                        v.wait_ge(s_mm.h, mm0 + 2)
                        ub = UBs[:, (ci % 2) * 512:(ci % 2) * 512 + 512]
                        s_v.inc(v.tensor_tensor(
                            ub, U[:, ci * 512:(ci + 1) * 512], PBP[:], OP.mult))
                        v.tensor_tensor_scan(YS[:, ci * 512:(ci + 1) * 512],
                                             a, ub, H[:, ci:ci + 1],
                                             OP.mult, OP.add)
                        v.scalar_tensor_tensor(U[:, ci * 512:(ci + 1) * 512],
                                               U[:, ci * 512:(ci + 1) * 512],
                                               DPAR[:, ci:ci + 1],
                                               YS[:, ci * 512:(ci + 1) * 512],
                                               OP.mult, OP.add)
                        s_v.inc(v.tensor_tensor(SZ[:, ci * 512:(ci + 1) * 512],
                                                U[:, ci * 512:(ci + 1) * 512],
                                                SZ[:, ci * 512:(ci + 1) * 512],
                                                OP.mult))
                        vv(v, v.tensor_copy(H[:, ci:ci + 1],
                                            YS[:, ci * 512 + 511:ci * 512 + 512]))
                    emit("vector", m3v)
                    y_marks[ci] = s_v.n
                    if ci < 15:
                        emit_m3t(ci + 1)

                    def m3w(t, ci=ci):
                        t.wait_ge(*wo_waits[ci])
                        t.wait_ge(s_v.h, y_marks[ci])
                        w = wo_s.slot(wo_bidx + ci)
                        ins = None
                        for mo in range(4):
                            ins = t.matmul(PSA4[mo][:],
                                           w[:, mo * 128:mo * 128 + 128],
                                           SZ[:, ci * 512:(ci + 1) * 512],
                                           start=(ci == 0), stop=(ci == 15))
                            if ci == 15:
                                s_mm.inc(ins)
                        if ci < 15:
                            s_pec.inc(ins)
                    emit("tensor", m3w)

            if upto == "m3":
                return
            mm_half1 = s_mm.n
            y_ready = s_v.n
            with pse_w:
                PS4b = [pse_w.enter_context(
                    nc.psum_tensor(uname(f"PSO{i}"), [128, 512], F32))
                    for i in range(4)]
                PS = PSA4 + PS4b
                tiles = [d_wout[ci * 128:(ci + 1) * 128, :] for ci in range(16)]
                tile_waits, bidx = stream_loads(wo_s, tiles, 4)

                def fmm(t, tile_waits=tile_waits, bidx=bidx, y_ready=y_ready):
                    t.wait_ge(s_v.h, y_ready)
                    for ci in range(16):
                        t.wait_ge(*tile_waits[ci])
                        w = wo_s.slot(bidx + ci)
                        ins = None
                        for mo in range(4, 8):
                            ins = t.matmul(PS[mo][:],
                                           w[:, mo * 128:mo * 128 + 128],
                                           SZ[:, ci * 512:(ci + 1) * 512],
                                           start=(ci == 0), stop=(ci == 15))
                            if ci == 15:
                                s_mm.inc(ins)
                        if ci < 15:
                            s_pec.inc(ins)
                emit("tensor", fmm)
                mm_half2 = s_mm.n

                tr_done = {}
                for mo in range(8):
                    if upto == "m4a":
                        def m4x(s, mo=mo, mm0=mm0):
                            s.wait_ge(s_mm.h, mm0 + mo + 1)
                            s_ev.inc(s.copy(TM[:, 0:512], PS[mo][:]))
                        emit("scalar", m4x)
                        continue
                    evb = s_ev.n
                    tm = TM[:, (mo % 2) * 512:(mo % 2) * 512 + 512]

                    mmw = mm_half1 if mo < 4 else (mm_half2 - 4 + (mo - 4) + 1)

                    def m4s(s, mo=mo, mmw=mmw, tm=tm, trw=tr_done.get(mo - 2)):
                        s.wait_ge(s_mm.h, mmw)
                        if trw is not None:
                            s.wait_ge(s_mm.h, trw)
                        s_ev.inc(s.copy(tm, PS[mo][:]))
                    emit("scalar", m4s)

                    mmt = s_mm.n

                    def m4t(t, mo=mo, evb=evb, tm=tm):
                        t.wait_ge(s_ev.h, evb + 1)
                        for tt in range(TPC):
                            s_mm.inc(t.transpose(
                                PS[mo][:, tt * 128:tt * 128 + 128],
                                tm[:, tt * 128:tt * 128 + 128], IDF[:]))
                    emit("tensor", m4t)
                    tr_done[mo] = s_mm.n

                    def m4v(v, mo=mo, mmt=mmt):
                        v.wait_ge(s_mm.h, mmt + TPC)
                        ins = None
                        for tt in range(TPC):
                            xs = X[:, tt * D + mo * 128:tt * D + mo * 128 + 128]
                            ins = v.scalar_tensor_tensor(
                                xs, PS[mo][:, tt * 128:tt * 128 + 128],
                                0.0, xs, OP.bypass, OP.add)
                        s_v.inc(ins)
                    emit("vector", m4v)

    prologue_smalls()
    prologue_ffn_weights()
    scratch_ready = [(s.h, s.n) for s in _dbg_ts]

    if dbg == "w":
        with (nc.sbuf_tensor(uname("DBB"), [128, 4096], BF16) as DBB,
              nc.sbuf_tensor(uname("DBF"), [128, 8192], F32) as DBF):
            sq0 = s_sq.n
            tsv = [(s.h, s.n) for s in _dbg_ts]

            def dl(q):
                for h, nv in tsv:
                    q.wait_ge(h, nv)
                s_sq.inc(q.dma_start(out=DBB[:], in_=d_wgT[128:256, :]), 16)
            emit("sync", dl)

            def dv(v):
                v.wait_ge(s_sq.h, sq0 + 16)
                v.tensor_copy(DBF[:, 0:4096], DBB[:])
                v.tensor_copy(DBF[:, 4096:4100], SCW[:, 0:4])
                s_v.inc(v.tensor_copy(DBF[:, 4100:4104], SCW[:, 4:8]))
            emit("vector", dv)

            def dl2(q):
                s_sq.inc(q.dma_start(out=DBB[:, 0:1024], in_=d_wdT[0:128, 0:1024]), 16)
            emit("sync", dl2)
            sq1 = s_sq.n

            def dv2(v):
                v.wait_ge(s_sq.h, sq1)
                s_v.inc(v.tensor_copy(DBF[:, 4608:5632], DBB[:, 0:1024]))
            emit("vector", dv2)
            vfin = s_v.n

            def dst(g):
                g.wait_ge(s_v.h, vfin)
                s_st.inc(g.dma_start(out=d_dbg[:, 0:8192], in_=DBF[:]), 16)
                g.wait_ge(s_st.h, s_st.n)
            emit("gpsimd", dst)
        es.close()
        return nc

    wg_s = WStream(es, nc, "wg", None, 4, 1024)
    wu_s = WStream(es, nc, "wu", None, 4, 1024)
    wd_s = WStream(es, nc, "wd", None, 4, 1024)
    wi_s = WStream(es, nc, "wi", None, 4, 1024)
    wo_s = WStream(es, nc, "wo", None, 2, 1024)
    wg_s.first_waits = scratch_ready
    wu_s.first_waits = scratch_ready
    wd_s.first_waits = scratch_ready
    wi_s.first_waits = scratch_ready

    for k in range(NCH):
        xload_gq = load_x(k, first=(k == 0))
        if stage in ("full", "ffn1", "mamba"):
            ffn(k, xload_gq, (wg_s, wu_s, wd_s))
        if stage in ("full", "mamba") or stage.startswith("m"):
            mamba(k, (wi_s, wo_s),
                  upto=stage if stage.startswith("m") and stage != "mamba" else "m4")
        if stage == "full":
            ffn(k, xload_gq, (wg_s, wu_s, wd_s))
        store_x(k)

    def fin(g):
        g.wait_ge(s_st.h, s_st.n)
    emit("gpsimd", fin)

    es.close()
    return nc


def _prep_inputs(L0_attn_x, L0_attn_W_in, L0_attn_conv_w, L0_attn_W_x,
                 L0_attn_W_dt, L0_attn_b_dt, L0_attn_W_Bp, L0_attn_D_param,
                 L0_attn_W_out, L0_ffn_w_gate, L0_ffn_w_up, L0_ffn_w_down):
    import ml_dtypes
    f32 = np.float32
    x = np.ascontiguousarray(np.asarray(L0_attn_x, f32))
    w_dtb = np.zeros((48, DI), f32)
    w_dtb[0:16] = np.asarray(L0_attn_W_dt, f32)
    w_dtb[32:48] = np.asarray(L0_attn_W_Bp, f32)
    shared = dict(
        w_in=np.ascontiguousarray(np.asarray(L0_attn_W_in, f32)),
        w_x=np.ascontiguousarray(np.asarray(L0_attn_W_x, f32)),
        w_dtb=w_dtb,
        b_dt_col=np.ascontiguousarray(
            np.asarray(L0_attn_b_dt, f32).reshape(16, 128).T),
        d_col=np.ascontiguousarray(
            np.asarray(L0_attn_D_param, f32).reshape(16, 128).T),
        conv_col=np.ascontiguousarray(
            np.asarray(L0_attn_conv_w, f32).reshape(16, 128, 4)
            .transpose(1, 0, 2).reshape(128, 64)),
        w_out=np.ascontiguousarray(np.asarray(L0_attn_W_out, f32)),
        w_gate=np.ascontiguousarray(np.asarray(L0_ffn_w_gate, f32)),
        w_up=np.ascontiguousarray(np.asarray(L0_ffn_w_up, f32)),
        w_down=np.ascontiguousarray(np.asarray(L0_ffn_w_down, f32)),
        idf=np.eye(128, dtype=f32),
        idb=np.eye(128, dtype=ml_dtypes.bfloat16),
        onerow=np.ones((1, 128), f32),
        onecol=np.ones((128, 1), f32),
        consts=np.tile(np.array([[1e-5, 0.0, 0.0, 0.0]], f32), (128, 1)),
    )
    return x, shared


def kernel(**inputs):
    x, shared = _prep_inputs(**inputs)
    nc = build_program("full")
    in_maps = [dict(x_in=np.ascontiguousarray(x[c]), **shared) for c in range(8)]
    res = run_bass_kernel_spmd(nc, in_maps, core_ids=list(range(8)))
    return np.stack([np.asarray(r["y_out"], np.float32) for r in res.results],
                    axis=0)


# version 13
# speedup vs baseline: 1.0138x; 1.0049x over previous
import numpy as np
from contextlib import ExitStack

import concourse.bass as bass
import concourse.mybir as mybir
from concourse.bass_utils import run_bass_kernel_spmd

F32 = mybir.dt.float32
BF16 = mybir.dt.bfloat16
I32 = mybir.dt.int32
AX = mybir.AxisListType.X
OP = mybir.AluOpType
AF = mybir.ActivationFunctionType

L = 4096
D = 1024
DI = 2048
FI = 4096
T = 512
NCH = L // T
TPC = T // 128


class Sem:
    def __init__(self, es, nc, name):
        self.h = es.enter_context(nc.semaphore(name))
        self.n = 0

    def inc(self, instr, k=1):
        instr.then_inc(self.h, k)
        self.n += k
        return instr


class WStream:

    def __init__(self, es, nc, name, buf, slots, cols):
        self.buf, self.slots, self.cols = buf, slots, cols
        self.cons = []
        self.sems = [Sem(es, nc, f"{name}_s{i}") for i in range(slots)]
        self.first_waits = None

    def slot(self, i):
        c = (i % self.slots) * self.cols
        return self.buf[:, c:c + self.cols]


def build_program(stage="full", dbg=None):
    nc = bass.Bass("TRN2", target_bir_lowering=False,
                   detect_race_conditions=False)

    def din(name, shape, dtype=F32):
        return nc.dram_tensor(name, list(shape), dtype, kind="ExternalInput").ap()

    d_x = din("x_in", (L, D))
    d_win = din("w_in", (D, FI))
    d_wx = din("w_x", (DI, 32))
    d_wdtb = din("w_dtb", (48, DI))
    d_bdt = din("b_dt_col", (128, 16))
    d_dpar = din("d_col", (128, 16))
    d_conv = din("conv_col", (128, 64))
    d_wout = din("w_out", (DI, D))
    d_wg = din("w_gate", (FI, D))
    d_wu = din("w_up", (FI, D))
    d_wd = din("w_down", (D, FI))
    d_idf = din("idf", (128, 128))
    d_idb = din("idb", (128, 128), BF16)
    d_onerow = din("onerow", (1, 128))
    d_onecol = din("onecol", (128, 1))
    d_consts = din("consts", (128, 4))

    d_wgT = nc.dram_tensor("wgT", [D, FI], BF16).ap()
    d_wuT = nc.dram_tensor("wuT", [D, FI], BF16).ap()
    d_wdT = nc.dram_tensor("wdT", [FI, D], BF16).ap()
    d_winH = nc.dram_tensor("winH", [D, FI], BF16).ap()
    d_winL = nc.dram_tensor("winL", [D, FI], BF16).ap()
    d_y = nc.dram_tensor("y_out", [L, D], F32, kind="ExternalOutput").ap()
    d_dbg = nc.dram_tensor("dbg_out", [128, 16384], F32, kind="ExternalOutput").ap()

    es = ExitStack()
    s_gq = Sem(es, nc, "s_gq")
    s_sq = Sem(es, nc, "s_sq")
    s_pec = Sem(es, nc, "s_pec")
    s_mm = Sem(es, nc, "s_mm")
    s_ev = Sem(es, nc, "s_ev")
    s_v = Sem(es, nc, "s_v")
    s_sc = Sem(es, nc, "s_sc")
    s_st = Sem(es, nc, "s_st")

    _uid = [0]
    _dbg_ts = []

    def uname(n):
        _uid[0] += 1
        return f"{n}_{_uid[0]}"

    def sb(name, shape, dtype=F32):
        return es.enter_context(nc.sbuf_tensor(name, list(shape), dtype))

    X = sb("X", (128, TPC * D))
    STAT = sb("STAT", (128, 96))
    SCW = sb("SCW", (128, 8))
    H = sb("H", (128, 16))
    HIST = sb("HIST", (128, 48))
    BDT = sb("BDT", (128, 16))
    DPAR = sb("DPAR", (128, 16))
    CONVW = sb("CONVW", (128, 64))
    IDF = sb("IDF", (128, 128))
    IDB = sb("IDB", (128, 128), BF16)
    ONEROW = sb("ONEROW", (1, 128))
    ONECOL = sb("ONECOL", (128, 1))
    WXS = sb("WXS", (128, 512))
    WDTB = sb("WDTB", (48, DI))
    QADUM = sb("QADUM", (128, 1024))
    CONSTS = sb("CONSTS", (128, 4))

    block = es.enter_context(nc.Block())
    EN = dict(sync=block.sync, gpsimd=block.gpsimd, tensor=block.tensor,
              vector=block.vector, scalar=block.scalar)

    def emit(eng, fn):
        EN[eng](fn)

    C_MEAN, C_RSTD, C_AM, C_Q1, C_GSC, C_C2, C_S2, C_D1 = 0, 4, 8, 12, 16, 20, 24, 28
    C_C, C_VS, C_T1, C_T2, C_BND, C_AM2, C_S1 = 32, 36, 40, 44, 48, 52, 56

    def scol(base, tt):
        return STAT[:, base + tt:base + tt + 1]

    nqg = {"mm": 0}
    xlng = {"mm": 0}

    def vv(v, instr):
        s_v.inc(instr)
        v.wait_ge(s_v.h, s_v.n)

    def ss(s, instr):
        s_sc.inc(instr)
        s.wait_ge(s_sc.h, s_sc.n)

    def pewarm(t, PW, waits):
        for h, nv in waits:
            t.wait_ge(h, nv)
            t.matmul(PW[0:1, 0:1], ONECOL[:], ONECOL[:],
                     start=True, stop=True)

    def stream_loads(stream, tiles, n_groups, queue="sync", mm_final=None):
        pec_start = s_pec.n
        if mm_final is None:
            mm_final = s_mm.n + n_groups
        n = len(tiles)
        base_idx = len(stream.cons)
        for j in range(n - 1):
            stream.cons.append((s_pec.h, pec_start + j + 1))
        stream.cons.append((s_mm.h, mm_final))

        tile_waits = []
        cnt = {s: stream.sems[s].n for s in range(stream.slots)}
        for j in range(n):
            s = (base_idx + j) % stream.slots
            cnt[s] += 16
            tile_waits.append((stream.sems[s].h, cnt[s]))

        def f(q):
            if stream.first_waits is not None:
                for h, nv in stream.first_waits:
                    q.wait_ge(h, nv)
                stream.first_waits = None
            for j, t_ap in enumerate(tiles):
                gi = base_idx + j
                if gi >= stream.slots:
                    sem, val = stream.cons[gi - stream.slots]
                    q.wait_ge(sem, val)
                stream.sems[gi % stream.slots].inc(
                    q.dma_start(out=stream.slot(gi), in_=t_ap), 16)
        emit(queue, f)
        return tile_waits, base_idx

    def prologue_smalls():
        def f(g):
            for sbuf, dram in [(IDF, d_idf), (IDB, d_idb), (ONEROW, d_onerow),
                               (ONECOL, d_onecol), (BDT, d_bdt), (DPAR, d_dpar),
                               (CONVW, d_conv), (WDTB, d_wdtb),
                               (CONSTS, d_consts)]:
                s_gq.inc(g.dma_start(out=sbuf[:], in_=dram[:]), 16)
            for ci in range(16):
                s_gq.inc(g.dma_start(out=WXS[:, ci * 32:(ci + 1) * 32],
                                     in_=d_wx[ci * 128:(ci + 1) * 128, :]), 16)
        emit("gpsimd", f)
        gq_smalls = s_gq.n

        def w(e):
            e.wait_ge(s_gq.h, gq_smalls)
        for eng in ("tensor", "vector", "scalar"):
            emit(eng, w)

        def fv(v):
            v.memset(H[:], 0.0)
            v.memset(HIST[:], 0.0)
            s_v.inc(v.memset(STAT[:], 0.0))
        emit("vector", fv)

    def prologue_ffn_weights():
        specs = [(d_wg, d_wgT, 32, D, 8, 0),
                 (d_wu, d_wuT, 32, D, 8, 1),
                 (d_wd, d_wdT, 8, FI, 32, 2)]
        pa = [Sem(es, nc, "pa0"), Sem(es, nc, "pa1")]
        ts = [Sem(es, nc, "ts0"), Sem(es, nc, "ts1")]
        _dbg_ts.extend(ts)
        with (nc.sbuf_tensor(uname("PTILE"), [128, 2 * FI], F32) as PTILE,
              nc.sbuf_tensor(uname("TERN"), [128, 32 * D], BF16) as TERN,
              nc.sbuf_tensor(uname("TROW"), [128, 2 * FI], BF16) as TROW,
              nc.sbuf_tensor(uname("QP1"), [128, FI], F32) as QP1,
              nc.sbuf_tensor(uname("QPI"), [128, FI], I32) as QPI,
              nc.sbuf_tensor(uname("QP2"), [128, FI], F32) as QP2,
              nc.sbuf_tensor(uname("ACC"), [128, 4], F32) as ACC,
              nc.psum_tensor(uname("PSA"), [128, 512], F32) as PSA,
              nc.psum_tensor(uname("PTB"), [128, 1024], BF16) as PTB):
            for d_in, d_out, ntile, tcols, orows, widx in specs:
                def pslot(i):
                    return PTILE[:, (i % 2) * tcols:(i % 2) * tcols + tcols]

                v_emit = s_v.n
                paw = [(pa[i % 2].h, pa[i % 2].n + 16 * (i // 2 + 1))
                       for i in range(ntile)]

                def fload(g, d_in=d_in, ntile=ntile, tcols=tcols,
                          v_emit=v_emit):
                    for i in range(ntile):
                        if i < 2:
                            g.wait_ge(s_v.h, v_emit)
                        else:
                            g.wait_ge(s_v.h, v_emit + 2 * (i - 2) + 1)
                        pa[i % 2].inc(g.dma_start(
                            out=PTILE[:, (i % 2) * tcols:(i % 2) * tcols + tcols],
                            in_=d_in[i * 128:(i + 1) * 128, :]), 16)
                emit("sync", fload)

                def fred(v, ntile=ntile, tcols=tcols, paw=paw):
                    for i in range(ntile):
                        v.wait_ge(*paw[i])
                        vv(v, v.reduce_sum(
                            ACC[:, 1:2],
                            PTILE[:, (i % 2) * tcols:(i % 2) * tcols + tcols],
                            axis=AX, apply_absolute_value=True))
                        if i == 0:
                            vv(v, v.tensor_copy(ACC[:, 0:1], ACC[:, 1:2]))
                        else:
                            vv(v, v.tensor_tensor(ACC[:, 0:1], ACC[:, 0:1],
                                                  ACC[:, 1:2], OP.add))
                emit("vector", fred)

                def fmm(t):
                    t.wait_ge(s_v.h, s_v.n)
                    s_mm.inc(t.matmul(PSA[0:1, 0:1], ACC[:, 0:1], ONECOL[:],
                                      start=True, stop=True))
                emit("tensor", fmm)

                nelem = float(ntile * 128 * tcols)
                mm_now = s_mm.n

                def ftiny(v, nelem=nelem, mm_now=mm_now):
                    v.wait_ge(s_mm.h, mm_now)
                    vv(v, v.tensor_scalar(ACC[0:1, 2:3], PSA[0:1, 0:1],
                                          1.0 / nelem, 1e-5, OP.mult, OP.max))
                    vv(v, v.reciprocal(ACC[0:1, 3:4], ACC[0:1, 2:3]))
                emit("vector", ftiny)

                def fbc(t):
                    t.wait_ge(s_v.h, s_v.n)
                    s_mm.inc(t.matmul(PSA[:, 1:2], ONEROW[:], ACC[0:1, 3:4],
                                      start=True, stop=True))
                emit("tensor", fbc)

                mm_now2 = s_mm.n

                def ftiny2(v, widx=widx, mm_now2=mm_now2):
                    v.wait_ge(s_mm.h, mm_now2)
                    vv(v, v.tensor_copy(SCW[:, 3 + widx:4 + widx], PSA[:, 1:2]))
                    vv(v, v.reciprocal(SCW[:, widx:widx + 1], PSA[:, 1:2]))
                emit("vector", ftiny2)

                v_emit2 = s_v.n
                mm_guard = s_mm.n
                paw2 = [(pa[i % 2].h, pa[i % 2].n + 16 * (i // 2 + 1))
                        for i in range(ntile)]

                def fload2(g, d_in=d_in, ntile=ntile, tcols=tcols,
                           v_emit2=v_emit2):
                    for i in range(ntile):
                        if i < 2:
                            g.wait_ge(s_v.h, v_emit2)
                        else:
                            g.wait_ge(s_v.h, v_emit2 + 2 * (i - 2) + 2)
                        pa[i % 2].inc(g.dma_start(
                            out=PTILE[:, (i % 2) * tcols:(i % 2) * tcols + tcols],
                            in_=d_in[i * 128:(i + 1) * 128, :]), 16)
                emit("sync", fload2)

                def fq(v, ntile=ntile, tcols=tcols, widx=widx,
                       paw2=paw2, mm_guard=mm_guard):
                    v.wait_ge(s_mm.h, mm_guard)
                    v.wait_ge(s_v.h, s_v.n)
                    for i in range(ntile):
                        v.wait_ge(*paw2[i])
                        src = PTILE[:, (i % 2) * tcols:(i % 2) * tcols + tcols]
                        v.tensor_scalar(QP1[:, :tcols], src,
                                        SCW[:, 3 + widx:4 + widx], 0.5,
                                        OP.mult, OP.add)
                        v.tensor_copy(QPI[:, :tcols], QP1[:, :tcols])
                        v.tensor_tensor(QP1[:, :tcols], QPI[:, :tcols],
                                        QP1[:, :tcols], OP.is_gt)
                        s_v.inc(v.tensor_tensor(QP2[:, :tcols], QPI[:, :tcols],
                                                QP1[:, :tcols], OP.subtract))
                        s_v.inc(v.tensor_scalar(
                            TERN[:, i * tcols:i * tcols + tcols],
                            QP2[:, :tcols], -1.0, 1.0, OP.max, OP.min))
                emit("vector", fq)

                if dbg == "tern" and widx == 0:
                    vfin0 = s_v.n

                    def dtap(v):
                        v.wait_ge(s_v.h, vfin0)
                        v.tensor_copy(QP1[:, 0:4096], TERN[:, 0:4096])
                        s_v.inc(v.tensor_copy(QP2[:, 0:4096], TERN[:, 4096:8192]))
                    emit("vector", dtap)
                    vfin = s_v.n

                    def dst(g):
                        g.wait_ge(s_v.h, vfin)
                        s_st.inc(g.dma_start(out=d_dbg[:, 0:4096], in_=QP1[:, 0:4096]), 16)
                        s_st.inc(g.dma_start(out=d_dbg[:, 4096:8192], in_=QP2[:, 0:4096]), 16)
                    emit("gpsimd", dst)

                st_marks = {}
                for j in range(orows):
                    vb = s_v.n
                    mmb = s_mm.n

                    scb_tr = s_sc.n

                    def ftr(t, j=j, ntile=ntile, tcols=tcols, vb=vb,
                            scb_tr=scb_tr):
                        t.wait_ge(s_v.h, vb)
                        t.wait_ge(s_sc.h, scb_tr)
                        for i in range(ntile):
                            if i >= 8:
                                t.wait_ge(s_sc.h, scb_tr + (i // 8))
                            s_mm.inc(t.transpose(
                                PTB[:, (i % 8) * 128:(i % 8) * 128 + 128],
                                TERN[:, i * tcols + j * 128:
                                     i * tcols + j * 128 + 128],
                                IDB[:]))
                    emit("tensor", ftr)

                    trow_half = (j % 2) * FI

                    tsnow = ts[j % 2].n

                    def fev(s, j=j, ntile=ntile, mmb=mmb, trow_half=trow_half,
                            st_marks=st_marks, tsnow=tsnow):
                        if j >= 2 and (j - 2) in st_marks:
                            s.wait_ge(*st_marks[j - 2])
                        elif j < 2:
                            s.wait_ge(ts[j % 2].h, tsnow)
                        ng = (ntile + 7) // 8
                        for gi in range(ng):
                            cnt = min(8, ntile - gi * 8)
                            s.wait_ge(s_mm.h, mmb + gi * 8 + cnt)
                            s_sc.inc(s.copy(
                                TROW[:, trow_half + gi * 1024:
                                     trow_half + gi * 1024 + cnt * 128],
                                PTB[:, :cnt * 128]))
                    emit("scalar", fev)

                    scnow = s_sc.n

                    def fst(g, j=j, d_out=d_out, ntile=ntile, scnow=scnow,
                            trow_half=trow_half):
                        g.wait_ge(s_sc.h, scnow)
                        ts[j % 2].inc(g.dma_start(
                            out=d_out[j * 128:(j + 1) * 128, :],
                            in_=TROW[:, trow_half:trow_half + ntile * 128]), 16)
                    emit("gpsimd", fst)
                    st_marks[j] = (ts[j % 2].h, ts[j % 2].n)

            v_emit3 = s_v.n
            paw3 = [(pa[i % 2].h, pa[i % 2].n + 16 * (i // 2 + 1))
                    for i in range(8)]

            def wload3(g, v_emit3=v_emit3):
                for i in range(8):
                    if i < 2:
                        g.wait_ge(s_v.h, v_emit3)
                    else:
                        g.wait_ge(s_v.h, v_emit3 + (i - 2) + 1)
                    pa[i % 2].inc(g.dma_start(
                        out=PTILE[:, (i % 2) * FI:(i % 2) * FI + FI],
                        in_=d_win[i * 128:(i + 1) * 128, :]), 16)
            emit("sync", wload3)

            st3 = {}
            for i in range(8):
                ts0n, ts1n = ts[0].n, ts[1].n
                srcw = PTILE[:, (i % 2) * FI:(i % 2) * FI + FI]

                def wsplitS(s, i=i, ts0n=ts0n, srcw=srcw, v_emit3=v_emit3):
                    s.wait_ge(*paw3[i])
                    if i >= 1:
                        s.wait_ge(*st3[i - 1][0])
                        s.wait_ge(s_v.h, v_emit3 + i)
                    else:
                        s.wait_ge(ts[0].h, ts0n)
                        s.wait_ge(s_v.h, v_emit3)
                    s_sc.inc(s.copy(TROW[:, 0:FI], srcw))
                    s_sc.inc(s.copy(QP1[:, :FI], TROW[:, 0:FI]))
                emit("scalar", wsplitS)

                scn3 = s_sc.n

                def wsplitV(v, i=i, ts1n=ts1n, srcw=srcw, scn3=scn3):
                    v.wait_ge(s_sc.h, scn3)
                    if i >= 1:
                        v.wait_ge(*st3[i - 1][1])
                    else:
                        v.wait_ge(ts[1].h, ts1n)
                    s_v.inc(v.tensor_tensor(TROW[:, FI:2 * FI], srcw,
                                            QP1[:, :FI], OP.subtract))
                emit("vector", wsplitV)

                vnow3 = s_v.n

                def wst3(g, i=i, vnow3=vnow3):
                    g.wait_ge(s_v.h, vnow3)
                    ts[0].inc(g.dma_start(out=d_winH[i * 128:(i + 1) * 128, :],
                                          in_=TROW[:, 0:FI]), 16)
                    ts[1].inc(g.dma_start(out=d_winL[i * 128:(i + 1) * 128, :],
                                          in_=TROW[:, FI:2 * FI]), 16)
                emit("gpsimd", wst3)
                st3[i] = [(ts[0].h, ts[0].n), (ts[1].h, ts[1].n)]

    def load_x(k, first):
        stn = s_st.n

        def f(g, k=k, first=first, stn=stn):
            if not first:
                g.wait_ge(s_st.h, stn)
            for tt in range(TPC):
                s_gq.inc(g.dma_start(
                    out=X[:, tt * D:(tt + 1) * D],
                    in_=d_x[k * T + tt * 128:k * T + tt * 128 + 128, :]), 16)
        emit("gpsimd", f)
        return s_gq.n

    def store_x(k):
        vn = s_v.n

        def f(g, k=k, vn=vn):
            g.wait_ge(s_v.h, vn)
            for tt in range(TPC):
                s_st.inc(g.dma_start(
                    out=d_y[k * T + tt * 128:k * T + tt * 128 + 128, :],
                    in_=X[:, tt * D:(tt + 1) * D]), 16)
        emit("gpsimd", f)

    def ffn(k, xload_gq, wstreams):
        wg_s, wu_s, wd_s = wstreams
        with (nc.sbuf_tensor(uname("WGb"), [128, 4 * 1024], BF16) as WGb,
              nc.sbuf_tensor(uname("WUb"), [128, 4 * 1024], BF16) as WUb,
              nc.sbuf_tensor(uname("WDb"), [128, 4 * 1024], BF16) as WDb,
              nc.sbuf_tensor(uname("PROD"), [128, TPC * FI], F32) as PROD,
              nc.sbuf_tensor(uname("QA"), [128, 1024], F32) as QA,
              nc.sbuf_tensor(uname("C32"), [128, 1024], I32) as C32,
              nc.sbuf_tensor(uname("QC"), [128, 1024], F32) as QC,
              nc.sbuf_tensor(uname("XCB"), [128, 1024], F32) as XCB,
              nc.sbuf_tensor(uname("SG"), [128, TPC * 1024], F32) as SG,
              nc.sbuf_tensor(uname("N1T"), [128, 4096], BF16) as N1T,
              nc.sbuf_tensor(uname("N2T"), [128, 16384], BF16) as N2T,
              nc.sbuf_tensor(uname("NQ"), [128, 4096], BF16) as NQ):
            wg_s.buf, wu_s.buf, wd_s.buf = WGb, WUb, WDb

            with (nc.psum_tensor(uname("PTB1"), [128, 1024], BF16) as PTB1,
                  nc.psum_tensor(uname("PW1"), [128, 8], F32) as PW1):
                for tt in range(TPC):
                    xs = X[:, tt * D:(tt + 1) * D]

                    def f1v(v, tt=tt, xs=xs):
                        v.wait_ge(s_sc.h, s_sc.n)
                        if tt == 0:
                            v.wait_ge(s_gq.h, xload_gq)
                        vv(v, v.reduce_sum(scol(C_T1, tt), xs, axis=AX))
                        vv(v, v.tensor_scalar(scol(C_MEAN, tt), scol(C_T1, tt),
                                              1.0 / D, None, OP.mult))
                        s_v.inc(v.tensor_scalar(XCB[:], xs, scol(C_MEAN, tt),
                                                None, OP.subtract))
                    emit("vector", f1v)
                    a1 = s_v.n

                    def f1s(s, tt=tt):
                        s.wait_ge(s_v.h, s_v.n)
                        ss(s, s.activation(QADUM[:], XCB[:], AF.Square,
                                           accum_out=scol(C_VS, tt)))
                        ss(s, s.activation(scol(C_T1, tt), scol(C_VS, tt),
                                           AF.Ln, bias=CONSTS[:, 0:1],
                                           scale=1.0 / D))
                        s_sc.inc(s.activation(scol(C_RSTD, tt), scol(C_T1, tt),
                                              AF.Exp, bias=0.0, scale=-0.5))
                    emit("scalar", f1s)
                    c1 = s_sc.n
                    b1 = s_v.n

                    def f1v2(v, tt=tt):
                        vv(v, v.reduce_max(scol(C_AM, tt), XCB[:], axis=AX,
                                           apply_absolute_value=True))
                        v.wait_ge(s_sc.h, s_sc.n)
                        vv(v, v.tensor_scalar(scol(C_AM2, tt), scol(C_AM, tt),
                                              scol(C_RSTD, tt), 1e-5,
                                              OP.mult, OP.max))
                        vv(v, v.reciprocal(scol(C_T1, tt), scol(C_AM2, tt)))
                        vv(v, v.tensor_scalar(scol(C_S1, tt), scol(C_T1, tt),
                                              127.0, None, OP.mult))
                        vv(v, v.tensor_scalar(scol(C_Q1, tt), scol(C_S1, tt),
                                              scol(C_RSTD, tt), None, OP.mult))
                        vv(v, v.reciprocal(scol(C_D1, tt), scol(C_S1, tt)))
                        vv(v, v.tensor_scalar(scol(C_GSC, tt), scol(C_D1, tt),
                                              SCW[:, 0:1], None, OP.mult))
                        vv(v, v.tensor_scalar(scol(C_C, tt), scol(C_D1, tt),
                                              SCW[:, 1:2], None, OP.mult))
                        vv(v, v.reciprocal(scol(C_T2, tt), scol(C_C, tt)))
                        vv(v, v.tensor_scalar(scol(C_BND, tt), scol(C_T2, tt),
                                              1e-5, None, OP.mult))
                        v.wait_ge(s_mm.h, nqg["mm"])
                        for hf in range(2):
                            c0 = hf * 512
                            v.tensor_scalar(QA[:, c0:c0 + 512],
                                            XCB[:, c0:c0 + 512],
                                            scol(C_Q1, tt), 0.5,
                                            OP.mult, OP.add)
                            v.tensor_copy(C32[:, c0:c0 + 512], QA[:, c0:c0 + 512])
                            v.tensor_tensor(QA[:, c0:c0 + 512], C32[:, c0:c0 + 512],
                                            QA[:, c0:c0 + 512], OP.is_gt)
                            s_v.inc(v.tensor_tensor(
                                NQ[:, c0:c0 + 512], C32[:, c0:c0 + 512],
                                QA[:, c0:c0 + 512], OP.subtract))
                    emit("vector", f1v2)

                    mmb = s_mm.n
                    evp1 = s_ev.n
                    vfin1 = s_v.n

                    def f1t(t, tt=tt, evp1=evp1, vfin1=vfin1,
                            a1=a1, b1=b1, c1=c1):
                        pewarm(t, PW1, [(s_v.h, max(0, a1 - 4)), (s_v.h, a1),
                                        (s_sc.h, c1), (s_v.h, b1 + 5),
                                        (s_v.h, b1 + 10)])
                        t.wait_ge(s_ev.h, evp1)
                        for hf in range(2):
                            t.wait_ge(s_v.h, vfin1 - 1 + hf)
                            for f in range(4):
                                fb = hf * 4 + f
                                s_mm.inc(t.transpose(
                                    PTB1[:, fb * 128:(fb + 1) * 128],
                                    NQ[:, fb * 128:(fb + 1) * 128], IDB[:]))
                    emit("tensor", f1t)
                    nqg["mm"] = s_mm.n

                    def f1e(s, tt=tt, mmb=mmb):
                        s.wait_ge(s_mm.h, mmb + 8)
                        s_ev.inc(s.copy(
                            N1T[:, tt * 1024:(tt + 1) * 1024], PTB1[:]))
                    emit("scalar", f1e)

            with ExitStack() as pse:
                PS = [pse.enter_context(
                    nc.psum_tensor(uname(f"PSF{i}"), [128, 512], F32)) for i in range(8)]

                n1t_ev = s_ev.n
                prev = {"evg": None, "vpr": None}
                for q in range(4):
                    for wsel in ("g", "u"):
                        stream = wg_s if wsel == "g" else wu_s
                        dscr = d_wgT if wsel == "g" else d_wuT
                        tiles = [dscr[f * 128:(f + 1) * 128, q * 1024:(q + 1) * 1024]
                                 for f in range(8)]
                        tile_waits, bidx = stream_loads(stream, tiles, 8)

                        vb = s_v.n
                        mm0 = s_mm.n
                        evg, vpr = prev["evg"], prev["vpr"]

                        def fmm(t, q=q, wsel=wsel, stream=stream, bidx=bidx,
                                tile_waits=tile_waits, evg=evg, vpr=vpr, vb=vb,
                                n1t_ev=n1t_ev):
                            for f in range(8):
                                t.wait_ge(*tile_waits[f])
                                if f == 0 and wsel == "g" and q == 0:
                                    if vpr is None:
                                        t.wait_ge(s_v.h, vb)
                                    t.wait_ge(s_ev.h, n1t_ev)
                                w = stream.slot(bidx + f)
                                ins = None
                                for tt in range(TPC):
                                    lhsT = N1T[:, tt * 1024 + f * 128:
                                               tt * 1024 + f * 128 + 128]
                                    for ncn in range(2):
                                        g = tt * 2 + ncn
                                        if f == 0:
                                            if wsel == "u":
                                                t.wait_ge(s_ev.h, evg + g + 1)
                                            elif vpr is not None:
                                                t.wait_ge(s_v.h, vpr + g + 1)
                                        ins = t.matmul(
                                            PS[g][:], lhsT,
                                            w[:, ncn * 512:ncn * 512 + 512],
                                            start=(f == 0), stop=(f == 7))
                                        if f == 7:
                                            s_mm.inc(ins)
                                if f < 7:
                                    s_pec.inc(ins)
                        emit("tensor", fmm)

                        if wsel == "g":
                            prev["evg"] = s_ev.n
                            def fev(s, mm0=mm0):
                                for g in range(8):
                                    s.wait_ge(s_mm.h, mm0 + g + 1)
                                    tt, ncn = g // 2, g % 2
                                    s_ev.inc(s.activation(
                                        SG[:, tt * 1024 + ncn * 512:
                                           tt * 1024 + ncn * 512 + 512],
                                        PS[g][:], AF.Silu, bias=0.0,
                                        scale=scol(C_GSC, tt)))
                            emit("scalar", fev)
                        else:
                            prev["vpr"] = s_v.n

                            def fpr(v, q=q, mm0=mm0):
                                for g in range(8):
                                    v.wait_ge(s_mm.h, mm0 + g + 1)
                                    tt, ncn = g // 2, g % 2
                                    s_v.inc(v.tensor_tensor(
                                        PROD[:, tt * FI + q * 1024 + ncn * 512:
                                             tt * FI + q * 1024 + ncn * 512 + 512],
                                        SG[:, tt * 1024 + ncn * 512:
                                           tt * 1024 + ncn * 512 + 512],
                                        PS[g][:], OP.mult))
                                for tt in range(TPC):
                                    vv(v, v.reduce_max(
                                        STAT[:, 64 + tt * 4 + q:65 + tt * 4 + q],
                                        PROD[:, tt * FI + q * 1024:
                                             tt * FI + q * 1024 + 1024],
                                        axis=AX, apply_absolute_value=True))
                            emit("vector", fpr)

            if dbg == "prod" and k == 0:
                def dvp(v):
                    s_v.inc(v.tensor_copy(QADUM[:], PROD[:, 0:1024]))
                emit("vector", dvp)
                vfin = s_v.n

                def dstp(g):
                    g.wait_ge(s_v.h, vfin)
                    s_st.inc(g.dma_start(out=d_dbg[:, 0:1024], in_=QADUM[:]), 16)
                    g.wait_ge(s_st.h, s_st.n)
                emit("gpsimd", dstp)

            with nc.psum_tensor(uname("PTB2"), [128, 1024], BF16) as PTB2:
                def f3c(v):
                    w4 = lambda b: STAT[:, b:b + 4]
                    vv(v, v.reduce_max(w4(C_T1),
                                       STAT[:, 64:80].rearrange(
                                           "p (t q) -> p t q", q=4),
                                       axis=AX, apply_absolute_value=False))
                    vv(v, v.tensor_tensor(w4(C_AM2), w4(C_T1), w4(C_BND),
                                          OP.max))
                    vv(v, v.reciprocal(w4(C_T2), w4(C_AM2)))
                    vv(v, v.tensor_scalar(w4(C_S2), w4(C_T2),
                                          127.0, None, OP.mult))
                    vv(v, v.reciprocal(w4(C_T1), w4(C_S2)))
                    vv(v, v.tensor_tensor(w4(C_T2), w4(C_T1), w4(C_C),
                                          OP.mult))
                    vv(v, v.tensor_scalar(w4(C_C2), w4(C_T2),
                                          SCW[:, 2:3], 0.5, OP.mult, OP.mult))
                emit("vector", f3c)

                for tt in range(TPC):
                    pre3 = s_v.n

                    def f3v(v, tt=tt):
                        pr = PROD[:, tt * FI:(tt + 1) * FI]
                        v.wait_ge(s_mm.h, nqg["mm"])
                        for sl in range(4):
                            p = pr[:, sl * 1024:(sl + 1) * 1024]
                            v.tensor_scalar(QA[:], p, scol(C_S2, tt), 0.5,
                                            OP.mult, OP.add)
                            v.tensor_copy(C32[:], QA[:])
                            v.tensor_tensor(QA[:], C32[:], QA[:], OP.is_gt)
                            s_v.inc(v.tensor_tensor(
                                NQ[:, sl * 1024:(sl + 1) * 1024],
                                C32[:], QA[:], OP.subtract))
                    emit("vector", f3v)

                    mmb = s_mm.n
                    evb2 = s_ev.n

                    def f3t(t, tt=tt, pre3=pre3, evb2=evb2):
                        t.wait_ge(s_ev.h, evb2)
                        for blk in range(4):
                            t.wait_ge(s_v.h, pre3 + 1 + blk)
                            if blk > 0:
                                t.wait_ge(s_ev.h, evb2 + blk)
                            for f in range(8):
                                s_mm.inc(t.transpose(
                                    PTB2[:, f * 128:(f + 1) * 128],
                                    NQ[:, blk * 1024 + f * 128:
                                       blk * 1024 + f * 128 + 128], IDB[:]))
                    emit("tensor", f3t)
                    nqg["mm"] = s_mm.n

                    def f3e(s, tt=tt, mmb=mmb):
                        for blk in range(4):
                            s.wait_ge(s_mm.h, mmb + blk * 8 + 8)
                            s_ev.inc(s.copy(
                                N2T[:, tt * 4096 + blk * 1024:
                                    tt * 4096 + blk * 1024 + 1024], PTB2[:]))
                    emit("scalar", f3e)

            n2t_ready = s_v.n
            n2t_ev = s_ev.n
            with ExitStack() as pse:
                PS = [pse.enter_context(
                    nc.psum_tensor(uname(f"PSD{i}"), [128, 512], F32)) for i in range(8)]

                tiles = [d_wdT[f2 * 128:(f2 + 1) * 128, :] for f2 in range(32)]
                tile_waits, bidx = stream_loads(wd_s, tiles, 8)

                def fmm(t, tile_waits=tile_waits, bidx=bidx,
                        n2t_ready=n2t_ready, n2t_ev=n2t_ev):
                    t.wait_ge(s_v.h, n2t_ready)
                    t.wait_ge(s_ev.h, n2t_ev)
                    for f2 in range(32):
                        t.wait_ge(*tile_waits[f2])
                        w = wd_s.slot(bidx + f2)
                        ins = None
                        for tt in range(TPC):
                            lhsT = N2T[:, tt * 4096 + f2 * 128:
                                       tt * 4096 + f2 * 128 + 128]
                            for ncn in range(2):
                                ins = t.matmul(
                                    PS[tt * 2 + ncn][:], lhsT,
                                    w[:, ncn * 512:ncn * 512 + 512],
                                    start=(f2 == 0), stop=(f2 == 31))
                                if f2 == 31:
                                    s_mm.inc(ins)
                        if f2 < 31:
                            s_pec.inc(ins)
                emit("tensor", fmm)

                mm0 = s_mm.n - 8

                def fres(v, mm0=mm0):
                    for g in range(8):
                        v.wait_ge(s_mm.h, mm0 + g + 1)
                        tt, ncn = g // 2, g % 2
                        xs = X[:, tt * D + ncn * 512:tt * D + ncn * 512 + 512]
                        s_v.inc(v.scalar_tensor_tensor(
                            xs, PS[g][:], scol(C_C2, tt), xs, OP.mult, OP.add))
                emit("vector", fres)

    def mamba(k, wstreams, upto="m4"):
        wi_s, wo_s = wstreams
        with (nc.sbuf_tensor(uname("WIb"), [128, 4 * 1024], BF16) as WIb,
              nc.sbuf_tensor(uname("WOb"), [128, 2 * 1024], F32) as WOb,
              nc.sbuf_tensor(uname("XCT"), [128, 16 * 515], F32) as XCT,
              nc.sbuf_tensor(uname("SZ"), [128, 16 * 512], F32) as SZ,
              nc.sbuf_tensor(uname("U"), [128, 16 * 512], F32) as U,
              nc.sbuf_tensor(uname("YS"), [128, 16 * 512], F32) as YS,
              nc.sbuf_tensor(uname("XLTH"), [128, 8 * 512], BF16) as XLTH,
              nc.sbuf_tensor(uname("XLTL"), [128, 8 * 512], BF16) as XLTL,
              nc.sbuf_tensor(uname("UBs"), [128, 2 * 512], F32) as UBs,
              nc.sbuf_tensor(uname("XPS"), [48, 512], F32) as XPS,
              nc.sbuf_tensor(uname("TM"), [128, 2 * 512], F32) as TM,
              nc.sbuf_tensor(uname("XLN"), [128, 1024], F32) as XLN):
            wi_s.buf, wo_s.buf = WIb, WOb
            EXPS = QADUM

            def xct(ci):
                return XCT[:, ci * 515:ci * 515 + 515]

            with (nc.psum_tensor(uname("PTF"), [128, 1024], F32) as PTF,
                  nc.psum_tensor(uname("PW4"), [128, 8], F32) as PW4):
                for tt in range(TPC):
                    xs = X[:, tt * D:(tt + 1) * D]

                    def m0v(v, tt=tt, xs=xs):
                        v.wait_ge(s_sc.h, s_sc.n)
                        v.wait_ge(s_mm.h, xlng["mm"])
                        vv(v, v.reduce_sum(scol(C_T1, tt), xs, axis=AX))
                        vv(v, v.tensor_scalar(scol(C_MEAN, tt), scol(C_T1, tt),
                                              1.0 / D, None, OP.mult))
                        s_v.inc(v.tensor_scalar(XLN[:], xs, scol(C_MEAN, tt),
                                                None, OP.subtract))
                    emit("vector", m0v)
                    a2 = s_v.n

                    def m0s(s, tt=tt):
                        s.wait_ge(s_v.h, s_v.n)
                        ss(s, s.activation(QADUM[:], XLN[:], AF.Square,
                                           accum_out=scol(C_VS, tt)))
                        ss(s, s.activation(scol(C_T1, tt), scol(C_VS, tt),
                                           AF.Ln, bias=CONSTS[:, 0:1],
                                           scale=1.0 / D))
                        s_sc.inc(s.activation(scol(C_RSTD, tt), scol(C_T1, tt),
                                              AF.Exp, bias=0.0, scale=-0.5))
                    emit("scalar", m0s)
                    c2m = s_sc.n

                    def m0v2(v, tt=tt):
                        v.wait_ge(s_sc.h, s_sc.n)
                        s_v.inc(v.tensor_scalar(XLN[:], XLN[:],
                                                scol(C_RSTD, tt), None, OP.mult))
                    emit("vector", m0v2)

                    mmb = s_mm.n
                    evp = s_ev.n

                    def m0t(t, tt=tt, evp=evp, a2=a2, c2m=c2m, vfin=s_v.n):
                        pewarm(t, PW4, [(s_v.h, max(0, a2 - 6)),
                                        (s_v.h, max(0, a2 - 3)), (s_v.h, a2),
                                        (s_sc.h, c2m)])
                        t.wait_ge(s_v.h, vfin)
                        t.wait_ge(s_ev.h, evp)
                        for f in range(8):
                            s_mm.inc(t.transpose(PTF[:, f * 128:(f + 1) * 128],
                                                 XLN[:, f * 128:(f + 1) * 128],
                                                 IDF[:]))
                    emit("tensor", m0t)
                    xlng["mm"] = s_mm.n

                    def m0e(s, tt=tt, mmb=mmb):
                        s.wait_ge(s_mm.h, mmb + 8)
                        dst = XLTH.rearrange("p (f c) -> p f c", c=512)[
                            :, :, tt * 128:tt * 128 + 128]
                        src = PTF.rearrange("p (f c) -> p f c", c=128)
                        s_ev.inc(s.copy(dst, src))
                    emit("scalar", m0e)

                    evh = s_ev.n

                    def m0l(v, tt=tt, evh=evh):
                        v.wait_ge(s_ev.h, evh)
                        dstH = XLTH.rearrange("p (f c) -> p f c", c=512)[
                            :, :, tt * 128:tt * 128 + 128]
                        dstL = XLTL.rearrange("p (f c) -> p f c", c=512)[
                            :, :, tt * 128:tt * 128 + 128]
                        qv = QADUM.rearrange("p (f c) -> p f c", c=128)
                        src = PTF.rearrange("p (f c) -> p f c", c=128)
                        s_v.inc(v.tensor_copy(qv, dstH))
                        s_v.inc(v.tensor_tensor(dstL, src, qv, OP.subtract))
                    emit("vector", m0l)

            if upto == "m0":
                return
            xc_ev_marks = {}
            xlt_ready = s_v.n
            xlt_ev = s_ev.n
            with ExitStack() as pse:
                PS = [pse.enter_context(
                    nc.psum_tensor(uname(f"PSM{i}"), [128, 512], F32)) for i in range(8)]

                for mg in range(4):
                    tiles = []
                    for f in range(8):
                        tiles.append(d_winH[f * 128:(f + 1) * 128,
                                            mg * 1024:(mg + 1) * 1024])
                        tiles.append(d_winL[f * 128:(f + 1) * 128,
                                            mg * 1024:(mg + 1) * 1024])
                    tile_waits, bidx = stream_loads(wi_s, tiles, 8)
                    evb = s_ev.n

                    def fmm(t, mg=mg, tile_waits=tile_waits, bidx=bidx, evb=evb,
                            xlt_ready=xlt_ready, xlt_ev=xlt_ev):
                        if mg == 0:
                            t.wait_ge(s_v.h, xlt_ready)
                            t.wait_ge(s_ev.h, xlt_ev)
                        else:
                            t.wait_ge(s_ev.h, evb)
                        for f in range(8):
                            t.wait_ge(*tile_waits[2 * f])
                            t.wait_ge(*tile_waits[2 * f + 1])
                            wh = wi_s.slot(bidx + 2 * f)
                            wl = wi_s.slot(bidx + 2 * f + 1)
                            rh = XLTH[:, f * 512:(f + 1) * 512]
                            rl = XLTL[:, f * 512:(f + 1) * 512]
                            for m8 in range(8):
                                whc = wh[:, m8 * 128:m8 * 128 + 128]
                                wlc = wl[:, m8 * 128:m8 * 128 + 128]
                                t.matmul(PS[m8][:], whc, rh,
                                         start=(f == 0), stop=False)
                                i2 = t.matmul(PS[m8][:], whc, rl,
                                              start=False, stop=False)
                                i3 = t.matmul(PS[m8][:], wlc, rh,
                                              start=False, stop=(f == 7))
                                if f == 7:
                                    s_mm.inc(i3)
                                elif m8 == 7:
                                    s_pec.inc(i3)
                                if m8 == 7:
                                    s_pec.inc(i2)
                    emit("tensor", fmm)

                    mm0 = s_mm.n - 8

                    def fev(s, mg=mg, mm0=mm0):
                        for m8 in range(8):
                            m = mg * 8 + m8
                            s.wait_ge(s_mm.h, mm0 + m8 + 1)
                            if m < 16:
                                s_ev.inc(s.copy(xct(m)[:, 3:515], PS[m8][:]))
                            else:
                                s_ev.inc(s.activation(
                                    SZ[:, (m - 16) * 512:(m - 15) * 512],
                                    PS[m8][:], AF.Silu))
                            if m < 16:
                                xc_ev_marks[m] = s_ev.n
                    emit("scalar", fev)

            if upto == "m1":
                return
            ev_all = s_ev.n
            v_conv0 = s_v.n
            sc_silu0 = s_sc.n

            def m2v(v, ev_all=ev_all, sc_silu0=sc_silu0):
                for ci in range(16):
                    v.wait_ge(s_ev.h, xc_ev_marks[ci])
                    if ci >= 2:
                        v.wait_ge(s_sc.h, sc_silu0 + (ci - 2) + 1)
                    xc = xct(ci)
                    e = EXPS[:, (ci % 2) * 512:(ci % 2) * 512 + 512]
                    vv(v, v.tensor_copy(xc[:, 0:3], HIST[:, ci * 3:ci * 3 + 3]))
                    cw = CONVW[:, ci * 4:ci * 4 + 4]
                    v.tensor_scalar(e, xc[:, 3:515], cw[:, 3:4], None, OP.mult)
                    v.scalar_tensor_tensor(e, xc[:, 2:514], cw[:, 2:3], e,
                                           OP.mult, OP.add)
                    v.scalar_tensor_tensor(e, xc[:, 1:513], cw[:, 1:2], e,
                                           OP.mult, OP.add)
                    s_v.inc(v.scalar_tensor_tensor(e, xc[:, 0:512], cw[:, 0:1],
                                                   e, OP.mult, OP.add))
                    v.tensor_copy(HIST[:, ci * 3:ci * 3 + 3], xc[:, 512:515])
            emit("vector", m2v)

            def m2s(s, v_conv0=v_conv0):
                for ci in range(16):
                    s.wait_ge(s_v.h, v_conv0 + 2 * ci + 2)
                    e = EXPS[:, (ci % 2) * 512:(ci % 2) * 512 + 512]
                    s_sc.inc(s.activation(U[:, ci * 512:(ci + 1) * 512],
                                          e, AF.Silu))
            emit("scalar", m2s)

            with nc.psum_tensor(uname("PXP"), [48, 512], F32) as PXP:
                sc_after = s_sc.n

                def m2t(t, sc_after=sc_after):
                    for ci in range(16):
                        t.wait_ge(s_sc.h, sc_after - (15 - ci))
                        s_mm.inc(t.matmul(PXP[0:16, :],
                                          WXS[:, ci * 32:ci * 32 + 16],
                                          U[:, ci * 512:(ci + 1) * 512],
                                          start=(ci == 0), stop=(ci == 15)))
                        s_mm.inc(t.matmul(PXP[32:48, :],
                                          WXS[:, ci * 32 + 16:ci * 32 + 32],
                                          U[:, ci * 512:(ci + 1) * 512],
                                          start=(ci == 0), stop=(ci == 15)))
                emit("tensor", m2t)

                mm_now = s_mm.n

                def m2e(v, mm_now=mm_now):
                    v.wait_ge(s_mm.h, mm_now)
                    vv(v, v.tensor_copy(XPS[0:16, :], PXP[0:16, :]))
                    vv(v, v.tensor_copy(XPS[32:48, :], PXP[32:48, :]))
                emit("vector", m2e)

            if upto == "m2":
                return
            pse_w = ExitStack()
            PSA4 = [pse_w.enter_context(
                nc.psum_tensor(uname(f"PSW{i}"), [128, 512], F32))
                for i in range(4)]
            with (nc.psum_tensor(uname("PDT"), [128, 512], F32) as PDT,
                  nc.psum_tensor(uname("PBP"), [128, 512], F32) as PBP):
                tiles = [d_wout[ci * 128:(ci + 1) * 128, :] for ci in range(16)]
                wo_waits, wo_bidx = stream_loads(wo_s, tiles, 4,
                                                 mm_final=s_mm.n + 36)
                y_marks = {}
                xp_ready = s_v.n
                mm0s = {}
                vpre = {}

                def emit_m3t(ci):
                    mm0s[ci] = s_mm.n
                    if ci == 0:
                        waits = [(s_v.h, xp_ready)]
                    else:
                        waits = [(s_sc.h, s_sc.n), (s_v.h, vpre[ci - 1] + 1)]

                    def m3t(t, ci=ci, waits=waits):
                        for h, nv in waits:
                            t.wait_ge(h, nv)
                        s_mm.inc(t.matmul(PDT[:],
                                          WDTB[0:16, ci * 128:ci * 128 + 128],
                                          XPS[0:16, :], start=True, stop=True))
                        s_mm.inc(t.matmul(PBP[:],
                                          WDTB[32:48, ci * 128:ci * 128 + 128],
                                          XPS[32:48, :], start=True, stop=True))
                    emit("tensor", m3t)

                emit_m3t(0)
                for ci in range(16):
                    mm0 = mm0s[ci]

                    def m3s(s, ci=ci, mm0=mm0):
                        s.wait_ge(s_mm.h, mm0 + 1)
                        s.activation(EXPS[:, 0:512], PDT[:], AF.Exp,
                                     bias=BDT[:, ci:ci + 1])
                        s_sc.inc(s.activation(xct(ci)[:, 3:515], EXPS[:, 0:512],
                                              AF.Ln, bias=1.0))
                    emit("scalar", m3s)

                    vpre[ci] = s_v.n

                    def m3v(v, ci=ci, mm0=mm0):
                        v.wait_ge(s_sc.h, s_sc.n)
                        a = xct(ci)[:, 3:515]
                        v.tensor_scalar(a, a, 1e-4, 1.0, OP.max, OP.min)
                        v.tensor_scalar(a, a, -1.0, 1.0, OP.mult, OP.add)
                        v.wait_ge(s_mm.h, mm0 + 2)
                        ub = UBs[:, (ci % 2) * 512:(ci % 2) * 512 + 512]
                        s_v.inc(v.tensor_tensor(
                            ub, U[:, ci * 512:(ci + 1) * 512], PBP[:], OP.mult))
                        v.tensor_tensor_scan(YS[:, ci * 512:(ci + 1) * 512],
                                             a, ub, H[:, ci:ci + 1],
                                             OP.mult, OP.add)
                        v.scalar_tensor_tensor(U[:, ci * 512:(ci + 1) * 512],
                                               U[:, ci * 512:(ci + 1) * 512],
                                               DPAR[:, ci:ci + 1],
                                               YS[:, ci * 512:(ci + 1) * 512],
                                               OP.mult, OP.add)
                        s_v.inc(v.tensor_tensor(SZ[:, ci * 512:(ci + 1) * 512],
                                                U[:, ci * 512:(ci + 1) * 512],
                                                SZ[:, ci * 512:(ci + 1) * 512],
                                                OP.mult))
                        vv(v, v.tensor_copy(H[:, ci:ci + 1],
                                            YS[:, ci * 512 + 511:ci * 512 + 512]))
                    emit("vector", m3v)
                    y_marks[ci] = s_v.n
                    if ci < 15:
                        emit_m3t(ci + 1)

                    def m3w(t, ci=ci):
                        t.wait_ge(*wo_waits[ci])
                        t.wait_ge(s_v.h, y_marks[ci])
                        w = wo_s.slot(wo_bidx + ci)
                        ins = None
                        for mo in range(4):
                            ins = t.matmul(PSA4[mo][:],
                                           w[:, mo * 128:mo * 128 + 128],
                                           SZ[:, ci * 512:(ci + 1) * 512],
                                           start=(ci == 0), stop=(ci == 15))
                            if ci == 15:
                                s_mm.inc(ins)
                        if ci < 15:
                            s_pec.inc(ins)
                    emit("tensor", m3w)

            if upto == "m3":
                return
            mm_half1 = s_mm.n
            y_ready = s_v.n
            with pse_w:
                PS4b = [pse_w.enter_context(
                    nc.psum_tensor(uname(f"PSO{i}"), [128, 512], F32))
                    for i in range(4)]
                PS = PSA4 + PS4b
                tiles = [d_wout[ci * 128:(ci + 1) * 128, :] for ci in range(16)]
                tile_waits, bidx = stream_loads(wo_s, tiles, 4)

                def fmm(t, tile_waits=tile_waits, bidx=bidx, y_ready=y_ready):
                    t.wait_ge(s_v.h, y_ready)
                    for ci in range(16):
                        t.wait_ge(*tile_waits[ci])
                        w = wo_s.slot(bidx + ci)
                        ins = None
                        for mo in range(4, 8):
                            ins = t.matmul(PS[mo][:],
                                           w[:, mo * 128:mo * 128 + 128],
                                           SZ[:, ci * 512:(ci + 1) * 512],
                                           start=(ci == 0), stop=(ci == 15))
                            if ci == 15:
                                s_mm.inc(ins)
                        if ci < 15:
                            s_pec.inc(ins)
                emit("tensor", fmm)
                mm_half2 = s_mm.n

                tr_done = {}
                for mo in range(8):
                    if upto == "m4a":
                        def m4x(s, mo=mo, mm0=mm0):
                            s.wait_ge(s_mm.h, mm0 + mo + 1)
                            s_ev.inc(s.copy(TM[:, 0:512], PS[mo][:]))
                        emit("scalar", m4x)
                        continue
                    evb = s_ev.n
                    tm = TM[:, (mo % 2) * 512:(mo % 2) * 512 + 512]

                    mmw = mm_half1 if mo < 4 else (mm_half2 - 4 + (mo - 4) + 1)

                    def m4s(s, mo=mo, mmw=mmw, tm=tm, trw=tr_done.get(mo - 2)):
                        s.wait_ge(s_mm.h, mmw)
                        if trw is not None:
                            s.wait_ge(s_mm.h, trw)
                        s_ev.inc(s.copy(tm, PS[mo][:]))
                    emit("scalar", m4s)

                    mmt = s_mm.n

                    def m4t(t, mo=mo, evb=evb, tm=tm):
                        t.wait_ge(s_ev.h, evb + 1)
                        for tt in range(TPC):
                            s_mm.inc(t.transpose(
                                PS[mo][:, tt * 128:tt * 128 + 128],
                                tm[:, tt * 128:tt * 128 + 128], IDF[:]))
                    emit("tensor", m4t)
                    tr_done[mo] = s_mm.n

                    def m4v(v, mo=mo, mmt=mmt):
                        v.wait_ge(s_mm.h, mmt + TPC)
                        ins = None
                        for tt in range(TPC):
                            xs = X[:, tt * D + mo * 128:tt * D + mo * 128 + 128]
                            ins = v.scalar_tensor_tensor(
                                xs, PS[mo][:, tt * 128:tt * 128 + 128],
                                0.0, xs, OP.bypass, OP.add)
                        s_v.inc(ins)
                    emit("vector", m4v)

    prologue_smalls()
    prologue_ffn_weights()
    scratch_ready = [(s.h, s.n) for s in _dbg_ts]

    if dbg == "w":
        with (nc.sbuf_tensor(uname("DBB"), [128, 4096], BF16) as DBB,
              nc.sbuf_tensor(uname("DBF"), [128, 8192], F32) as DBF):
            sq0 = s_sq.n
            tsv = [(s.h, s.n) for s in _dbg_ts]

            def dl(q):
                for h, nv in tsv:
                    q.wait_ge(h, nv)
                s_sq.inc(q.dma_start(out=DBB[:], in_=d_wgT[128:256, :]), 16)
            emit("sync", dl)

            def dv(v):
                v.wait_ge(s_sq.h, sq0 + 16)
                v.tensor_copy(DBF[:, 0:4096], DBB[:])
                v.tensor_copy(DBF[:, 4096:4100], SCW[:, 0:4])
                s_v.inc(v.tensor_copy(DBF[:, 4100:4104], SCW[:, 4:8]))
            emit("vector", dv)

            def dl2(q):
                s_sq.inc(q.dma_start(out=DBB[:, 0:1024], in_=d_wdT[0:128, 0:1024]), 16)
            emit("sync", dl2)
            sq1 = s_sq.n

            def dv2(v):
                v.wait_ge(s_sq.h, sq1)
                s_v.inc(v.tensor_copy(DBF[:, 4608:5632], DBB[:, 0:1024]))
            emit("vector", dv2)
            vfin = s_v.n

            def dst(g):
                g.wait_ge(s_v.h, vfin)
                s_st.inc(g.dma_start(out=d_dbg[:, 0:8192], in_=DBF[:]), 16)
                g.wait_ge(s_st.h, s_st.n)
            emit("gpsimd", dst)
        es.close()
        return nc

    wg_s = WStream(es, nc, "wg", None, 4, 1024)
    wu_s = WStream(es, nc, "wu", None, 4, 1024)
    wd_s = WStream(es, nc, "wd", None, 4, 1024)
    wi_s = WStream(es, nc, "wi", None, 4, 1024)
    wo_s = WStream(es, nc, "wo", None, 2, 1024)
    wg_s.first_waits = scratch_ready
    wu_s.first_waits = scratch_ready
    wd_s.first_waits = scratch_ready
    wi_s.first_waits = scratch_ready

    for k in range(NCH):
        xload_gq = load_x(k, first=(k == 0))
        if stage in ("full", "ffn1", "mamba"):
            ffn(k, xload_gq, (wg_s, wu_s, wd_s))
        if stage in ("full", "mamba") or stage.startswith("m"):
            mamba(k, (wi_s, wo_s),
                  upto=stage if stage.startswith("m") and stage != "mamba" else "m4")
        if stage == "full":
            ffn(k, xload_gq, (wg_s, wu_s, wd_s))
        store_x(k)

    def fin(g):
        g.wait_ge(s_st.h, s_st.n)
    emit("gpsimd", fin)

    es.close()
    return nc


def _prep_inputs(L0_attn_x, L0_attn_W_in, L0_attn_conv_w, L0_attn_W_x,
                 L0_attn_W_dt, L0_attn_b_dt, L0_attn_W_Bp, L0_attn_D_param,
                 L0_attn_W_out, L0_ffn_w_gate, L0_ffn_w_up, L0_ffn_w_down):
    import ml_dtypes
    f32 = np.float32
    x = np.ascontiguousarray(np.asarray(L0_attn_x, f32))
    w_dtb = np.zeros((48, DI), f32)
    w_dtb[0:16] = np.asarray(L0_attn_W_dt, f32)
    w_dtb[32:48] = np.asarray(L0_attn_W_Bp, f32)
    shared = dict(
        w_in=np.ascontiguousarray(np.asarray(L0_attn_W_in, f32)),
        w_x=np.ascontiguousarray(np.asarray(L0_attn_W_x, f32)),
        w_dtb=w_dtb,
        b_dt_col=np.ascontiguousarray(
            np.asarray(L0_attn_b_dt, f32).reshape(16, 128).T),
        d_col=np.ascontiguousarray(
            np.asarray(L0_attn_D_param, f32).reshape(16, 128).T),
        conv_col=np.ascontiguousarray(
            np.asarray(L0_attn_conv_w, f32).reshape(16, 128, 4)
            .transpose(1, 0, 2).reshape(128, 64)),
        w_out=np.ascontiguousarray(np.asarray(L0_attn_W_out, f32)),
        w_gate=np.ascontiguousarray(np.asarray(L0_ffn_w_gate, f32)),
        w_up=np.ascontiguousarray(np.asarray(L0_ffn_w_up, f32)),
        w_down=np.ascontiguousarray(np.asarray(L0_ffn_w_down, f32)),
        idf=np.eye(128, dtype=f32),
        idb=np.eye(128, dtype=ml_dtypes.bfloat16),
        onerow=np.ones((1, 128), f32),
        onecol=np.ones((128, 1), f32),
        consts=np.tile(np.array([[1e-5, 0.0, 0.0, 0.0]], f32), (128, 1)),
    )
    return x, shared


def kernel(**inputs):
    x, shared = _prep_inputs(**inputs)
    nc = build_program("full")
    in_maps = [dict(x_in=np.ascontiguousarray(x[c]), **shared) for c in range(8)]
    res = run_bass_kernel_spmd(nc, in_maps, core_ids=list(range(8)))
    return np.stack([np.asarray(r["y_out"], np.float32) for r in res.results],
                    axis=0)
